# version 03 (final)
# baseline (speedup 1.0000x reference)
import contextlib
import numpy as np
import ml_dtypes
import concourse.bass as bass
import concourse.mybir as mybir
from concourse.bass_utils import run_bass_kernel_spmd

F32 = mybir.dt.float32
BF16 = mybir.dt.bfloat16
I32 = mybir.dt.int32
AF = mybir.ActivationFunctionType
ALU = mybir.AluOpType
AX = mybir.AxisListType

ENGINES = ("sync", "scalar", "vector", "tensor", "gpsimd")

NCORES = 8
B = 2
T = 8192
DM = 2048
TL = 2048
H = 32
DH = 64
WIN = 512
NBLK = TL // WIN
E = 8
INTER = 1024
DI = 4096
DS = 16
DCONV = 4
DTRANK = 128
HID = 8192
BH = 5632
CHPC = DI // NCORES
BHPC = BH // NCORES
DMPC = DM // NCORES
M0 = 20.0
RG = [[0, 1, 2, 3, 4, 5, 6, 7]]

FULL = dict(stage=99, debug_out=None)
C_STAGE = None
PROFILE_STAGES = False


class _Scope:
    def __init__(self, p):
        self.p = p
        self.st = contextlib.ExitStack()

    def __enter__(self):
        self.p._scope_stack.append(self.st)
        return self

    def __exit__(self, *a):
        self.p._scope_stack.pop()
        self.st.close()
        return False


class Prog:

    def __init__(self, nc):
        self.nc = nc
        self.stream = {e: [] for e in ENGINES}
        self.sems = {}
        self.counts = {}
        self.stack = contextlib.ExitStack()
        self._scope_stack = []
        self.grab_target = None
        self.grab_dram = None

    def _cur_stack(self):
        return self._scope_stack[-1] if self._scope_stack else self.stack

    def sem(self, name):
        if name in self.sems:
            return self.sems[name]
        h = self.stack.enter_context(self.nc.semaphore(name))
        self.sems[name] = h
        self.counts[name] = 0
        return h

    def _uniq(self, name):
        if not hasattr(self, "_names"):
            self._names = {}
        n = self._names.get(name, 0)
        self._names[name] = n + 1
        return name if n == 0 else f"{name}_{n}"

    def sb(self, name, shape, dt):
        return self._cur_stack().enter_context(self.nc.sbuf_tensor(self._uniq(name), shape, dt))

    def ps(self, name, shape, dt=F32):
        return self._cur_stack().enter_context(self.nc.psum_tensor(self._uniq(name), shape, dt))

    def wait(self, eng, sem, tgt):
        h = self.sems[sem]
        self.stream[eng].append(lambda e, h=h, t=tgt: e.wait_ge(h, t))

    def op(self, eng, builder, inc=None, amt=1):
        h = self.sems[inc] if inc else None

        def f(e, b=builder, h=h, a=amt):
            ins = b(e)
            if h is not None:
                ins.then_inc(h, a)
        self.stream[eng].append(f)
        if inc:
            self.counts[inc] += amt
            return self.counts[inc]
        return None

    def dma(self, eng, out, in_, inc=None, **kw):
        return self.op(eng, lambda e, o=out, i=in_: e.dma_start(out=o, in_=i, **kw),
                       inc=inc, amt=16)

    def seq(self, eng, builder, inc=None):
        s = eng + "_self"
        self.sem(s)
        c = self.op(eng, builder, inc=s)
        if inc:
            self.counts[inc] += 1
            h = self.sems[inc]
            self.stream[eng].append(lambda e, h=h: e.sem_inc(h, 1))
        self.wait(eng, s, c)
        return self.counts[inc] if inc else c

    def scope(self):
        return self.stack.enter_context(contextlib.ExitStack()) if False else _Scope(self)

    def barrier(self, engines=ENGINES):
        self.sem("bar")
        h = self.sems["bar"]
        tgt = self.counts["bar"] + len(engines)
        for e in engines:
            self.stream[e].append(lambda eng, h=h: eng.sem_inc(h, 1))
        self.counts["bar"] = tgt
        for e in engines:
            self.stream[e].append(lambda eng, h=h, t=tgt: eng.wait_ge(h, t))

    def stage(self, name):
        if PROFILE_STAGES:
            for e in ENGINES:
                self.stream[e].append(("scope", name))

    def _run_stream(self, e, items):
        nc = self.nc
        cur = None
        for f in items:
            if isinstance(f, tuple) and f[0] == "scope":
                if cur is not None:
                    nc.leave_named_scope(cur[0], cur[1], False)
                sid, _ = nc.enter_named_scope(f[1], False)
                cur = (f[1], sid)
            else:
                f(e)
        if cur is not None:
            nc.leave_named_scope(cur[0], cur[1], False)

    def finalize(self):
        nc = self.nc
        with nc.Block() as block:
            @block.sync
            def _(e):
                self._run_stream(e, self.stream["sync"])

            @block.scalar
            def _(e):
                self._run_stream(e, self.stream["scalar"])

            @block.vector
            def _(e):
                self._run_stream(e, self.stream["vector"])

            @block.tensor
            def _(e):
                self._run_stream(e, self.stream["tensor"])

            @block.gpsimd
            def _(e):
                self._run_stream(e, self.stream["gpsimd"])
        self.stack.close()


def cdiv(a, b):
    return (a + b - 1) // b


def emit_ln_T_to_D(p, src_hbm_ap, n_tok, dst_dram, idb, cst, pref, lo_dram=None):
    ld, st, tp, ev = pref + "_ld", pref + "_st", pref + "_tp", pref + "_ev"
    ods = [pref + "_o0", pref + "_o1"]
    for s in (ld, st, tp, ev, *ods):
        p.sem(s)
    xt = p.sb(pref + "_xt", [128, DM], F32)
    xn_t = p.sb(pref + "_xn", [128, DM], BF16)
    xn_f = p.sb(pref + "_xf", [128, DM], F32)
    xn_lo = p.sb(pref + "_xl", [128, DM], BF16) if lo_dram is not None else None
    stl = p.sb(pref + "_sgl", [128, DM], BF16) if lo_dram is not None else None
    stats = p.sb(pref + "_sta", [128, 24], F32)
    aggr = p.sb(pref + "_agg", [128, 2], F32)
    rs = p.sb(pref + "_rs", [128, 1], F32)
    mm = p.sb(pref + "_mm", [128, 1], F32)
    ptp = [p.ps(pref + f"_p{j}", [128, 128], BF16) for j in range(4)]
    stg = [p.sb(pref + f"_sg{j}", [128, DM], BF16) for j in range(2)]
    sgdma = {0: 0, 1: 0}
    nt = cdiv(n_tok, 128)
    assert n_tok % 128 == 0
    for i in range(nt):
        r0 = i * 128
        r = min(128, n_tok - r0)
        if i > 0:
            p.wait("sync", ev, p.counts[ev])
        sg = stg[i % 2]
        if i >= 2:
            pc = sgdma[i % 2]
            p.wait("scalar", ods[i % 2], pc)
            p.wait("vector", ods[i % 2], pc)
        c = p.dma("sync", xt[0:r, :], src_hbm_ap[r0:r0 + r, :], inc=ld)
        p.wait("vector", ld, c)
        for j in range(4):
            p.seq("vector", lambda e, r=r, j=j: e.bn_stats(
                stats[0:r, 6 * j:6 * j + 6], xt[0:r, 512 * j:512 * (j + 1)]))
        p.op("vector", lambda e, r=r: e.bn_aggr(aggr[0:r, :], stats[0:r, :]), inc=st)
        cs = p.counts[st]
        p.wait("scalar", st, cs)
        p.seq("scalar", lambda e, r=r: e.activation(rs[0:r, :], aggr[0:r, 1:2], AF.Ln, bias=cst[0:r, 1:2]))
        p.seq("scalar", lambda e, r=r: e.activation(rs[0:r, :], rs[0:r, :], AF.Exp, scale=-0.5))
        p.seq("scalar", lambda e, r=r: e.activation(mm[0:r, :], aggr[0:r, 0:1], AF.Identity, scale=rs[0:r, 0:1]))
        p.seq("scalar", lambda e, r=r: e.activation(mm[0:r, :], mm[0:r, :], AF.Identity, scale=-1.0))
        p.seq("scalar", lambda e, r=r: e.activation(
            xn_f[0:r, :], xt[0:r, :], AF.Identity, bias=mm[0:r, 0:1], scale=rs[0:r, 0:1]))
        p.op("scalar", lambda e, r=r: e.copy(xn_t[0:r, :], xn_f[0:r, :]), inc=st)
        cs = p.counts[st]
        if lo_dram is not None:
            p.wait("vector", st, cs)
            p.op("vector", lambda e, r=r: e.tensor_sub(xn_lo[0:r, :], xn_f[0:r, :], xn_t[0:r, :]), inc=st)
            cs = p.counts[st]
        p.wait("tensor", st, cs)
        if not hasattr(p, "_ln_evtrack"):
            p._ln_evtrack = {}
        evt = p._ln_evtrack.setdefault(pref, {})
        subs = [(xn_t, sg)] + ([(xn_lo, stl)] if lo_dram is not None else [])
        si_ = 0
        for k in range(16):
            for src, dst in subs:
                slot = si_ % 4
                pj = ptp[slot]
                if evt.get(slot) is not None:
                    p.wait("tensor", ev, evt[slot])
                p.op("tensor", lambda e, k=k, r=r, pj=pj, src=src: e.transpose(
                    pj[:, 0:r], src[0:r, k * 128:(k + 1) * 128], idb[:, :]), inc=tp)
                ct = p.counts[tp]
                eng = "scalar" if (si_ % 2 == 0) else "vector"
                p.wait(eng, tp, ct)
                if eng == "scalar":
                    evt[slot] = p.op(eng, lambda e, k=k, r=r, pj=pj, dst=dst: e.copy(
                        dst[:, k * 128:k * 128 + r], pj[:, 0:r]), inc=ev)
                else:
                    evt[slot] = p.op(eng, lambda e, k=k, r=r, pj=pj, dst=dst: e.tensor_copy(
                        dst[:, k * 128:k * 128 + r], pj[:, 0:r]), inc=ev)
                si_ += 1
        p.wait("sync", ev, p.counts[ev])
        dview = dst_dram[:, i * 128:(i + 1) * 128].rearrange("(k q) c -> q k c", q=128)
        sgdma[i % 2] = p.dma("sync", dview, sg[:, :].rearrange("q (k c) -> q k c", c=128), inc=ods[i % 2])
        if lo_dram is not None:
            lview = lo_dram[:, i * 128:(i + 1) * 128].rearrange("(k q) c -> q k c", q=128)
            sgdma[i % 2] = p.dma("sync", lview, stl[:, :].rearrange("q (k c) -> q k c", c=128), inc=ods[i % 2])
    p.wait("sync", ods[0], p.counts[ods[0]])
    p.wait("sync", ods[1], p.counts[ods[1]])


def host_prep(inputs):
    f32 = np.float32
    asc = np.ascontiguousarray
    x = np.asarray(inputs["L0_attn_x"], f32)
    xf = x.reshape(B * T, DM)
    shared = {}
    shared["Wq"] = asc(inputs["L0_attn_Wq"])
    shared["Wk"] = asc(inputs["L0_attn_Wk"])
    shared["Wv"] = asc(inputs["L0_attn_Wv"])
    shared["Wo"] = asc(inputs["L0_attn_Wo"])
    shared["dog"] = asc(inputs["L0_attn_dog_params"])
    shared["sinks"] = asc(inputs["L0_attn_sinks"]).reshape(H, 1)
    shared["gateT"] = asc(np.transpose(inputs["L0_ffn_gate_w"]))
    shared["w1T"] = asc(np.transpose(inputs["L0_ffn_w1"], (0, 2, 1)))
    shared["w2T"] = asc(np.transpose(inputs["L0_ffn_w2"], (0, 2, 1)))
    shared["w3T"] = asc(np.transpose(inputs["L0_ffn_w3"], (0, 2, 1)))
    shared["ws1T"] = asc(np.transpose(inputs["L0_ffn_ws1"]))
    shared["ws2T"] = asc(np.transpose(inputs["L0_ffn_ws2"]))
    shared["ws3T"] = asc(np.transpose(inputs["L0_ffn_ws3"]))
    shared["ln1w"] = asc(inputs["L1_attn_ln1_w"]).reshape(DM, 1)
    shared["ln1b"] = asc(inputs["L1_attn_ln1_b"]).reshape(DM, 1)
    shared["ln2w"] = asc(inputs["L1_attn_ln2_w"]).reshape(DM, 1)
    shared["ln2b"] = asc(inputs["L1_attn_ln2_b"]).reshape(DM, 1)
    shared["inpT"] = asc(np.transpose(inputs["L1_attn_in_proj_w"]))
    shared["outpT"] = asc(np.transpose(inputs["L1_attn_out_proj_w"]))
    shared["ffpT"] = asc(np.transpose(inputs["L1_attn_ff_proj_w"]))
    shared["ffoT"] = asc(np.transpose(inputs["L1_attn_ff_out_w"]))
    shared["id_bf"] = np.eye(128, dtype=ml_dtypes.bfloat16)
    shared["id_rev"] = np.eye(128, dtype=ml_dtypes.bfloat16)[::-1].copy()
    shared["consts"] = np.array([-M0, 1e-5, 0.5e-8, 0.0], f32)
    shared["id_f32"] = np.eye(128, dtype=f32)

    xpwT = asc(np.transpose(inputs["L1_attn_x_proj_w"]))
    dtwT = asc(np.transpose(inputs["L1_attn_dt_proj_w"]))
    convw = asc(np.asarray(inputs["L1_attn_conv_w"])[:, 0, :])
    convb = asc(inputs["L1_attn_conv_b"]).reshape(DI, 1)
    dtb = asc(inputs["L1_attn_dt_proj_b"]).reshape(DI, 1)
    alog = asc(inputs["L1_attn_A_log"])
    dskip = asc(inputs["L1_attn_D_skip"]).reshape(DI, 1)
    bw1T = asc(np.transpose(inputs["L1_ffn_w1"]))
    bw2T = asc(np.transpose(inputs["L1_ffn_w2"]))
    bw3T = asc(np.transpose(inputs["L1_ffn_w3"]))

    per_core = []
    for c in range(NCORES):
        b = c // 4
        t0 = (c % 4) * TL
        g0 = b * T + t0
        d = dict(shared)
        d["x"] = asc(xf[g0:g0 + TL, :])
        if t0 == 0:
            d["xhalo"] = np.zeros((WIN, DM), f32)
            m0 = np.zeros((1024, WIN), f32)
            m0[WIN:, :] = 1.0
            d["mask0T"] = m0
        else:
            d["xhalo"] = asc(xf[g0 - WIN:g0, :])
            d["mask0T"] = np.ones((1024, WIN), f32)
        ch0 = c * CHPC
        d["dtwT_s"] = asc(dtwT[:, ch0:ch0 + CHPC])
        d["convw_s"] = asc(convw[ch0:ch0 + CHPC, :])
        d["convb_s"] = asc(convb[ch0:ch0 + CHPC, :])
        d["dtb_s"] = asc(dtb[ch0:ch0 + CHPC, :])
        d["alog_s"] = asc(alog[ch0:ch0 + CHPC, :])
        d["dskip_s"] = asc(dskip[ch0:ch0 + CHPC, :])
        d["xpwT_s"] = asc(xpwT[ch0:ch0 + CHPC, :])
        d["bw1T_s"] = asc(bw1T[c * DMPC:(c + 1) * DMPC, :])
        d["bw2T_s"] = asc(bw2T[c * DMPC:(c + 1) * DMPC, :])
        d["bw3T_s"] = asc(bw3T[c * BHPC:(c + 1) * BHPC, :])
        idxg = np.zeros((128, 32), np.int32)
        for dt_i in range(4):
            for r in range(8):
                for pp_ in range(128):
                    idxg[pp_, dt_i * 8 + r] = 4096 * r + ch0 + 128 * dt_i + pp_
        d["idxg"] = idxg
        idx2 = np.zeros((128, 32), np.int32)
        for kt in range(32):
            for pp_ in range(128):
                ch = 128 * kt + pp_
                idx2[pp_, kt] = 4096 * (ch // 512) + 8 * (ch % 512) + c
        d["idx2"] = idx2
        per_core.append(d)
    return per_core


def build(cfg=FULL):
    nc = bass.Bass("TRN2", target_bir_lowering=False, debug=False)
    p = Prog(nc)
    g = {}

    def din(name, shape, dt=F32):
        g[name] = nc.dram_tensor(name, shape, dt, kind="ExternalInput").ap()

    din("x", [TL, DM]); din("xhalo", [WIN, DM]); din("mask0T", [1024, WIN])
    din("Wq", [DM, DM]); din("Wk", [DM, DM]); din("Wv", [DM, DM]); din("Wo", [DM, DM])
    din("dog", [H, 7]); din("sinks", [H, 1])
    din("gateT", [DM, E])
    din("w1T", [E, DM, INTER]); din("w2T", [E, INTER, DM]); din("w3T", [E, DM, INTER])
    din("ws1T", [DM, INTER]); din("ws2T", [INTER, DM]); din("ws3T", [DM, INTER])
    din("ln1w", [DM, 1]); din("ln1b", [DM, 1]); din("ln2w", [DM, 1]); din("ln2b", [DM, 1])
    din("inpT", [DM, 2 * DI]); din("outpT", [DI, DM])
    din("ffpT", [DM, HID]); din("ffoT", [DI, DM])
    din("xpwT_s", [CHPC, 160]); din("dtwT_s", [DTRANK, CHPC])
    din("convw_s", [CHPC, DCONV]); din("convb_s", [CHPC, 1]); din("dtb_s", [CHPC, 1])
    din("alog_s", [CHPC, DS]); din("dskip_s", [CHPC, 1])
    din("bw1T_s", [DMPC, BH]); din("bw2T_s", [DMPC, BH]); din("bw3T_s", [BHPC, DM])
    din("id_bf", [128, 128], BF16); din("id_f32", [128, 128])
    din("id_rev", [128, 128], BF16); din("consts", [4])
    din("idxg", [128, 32], I32); din("idx2", [128, 32], I32)

    g["out"] = nc.dram_tensor("out", [TL, DM], F32, kind="ExternalOutput").ap()

    g["acc0"] = nc.dram_tensor("acc0", [DM, TL], F32, kind="Internal").ap()
    g["acc1"] = nc.dram_tensor("acc1", [DM, TL], F32, kind="Internal").ap()
    g["x1sp"] = nc.dram_tensor("x1sp", [DM, TL], F32, kind="Internal").ap()
    g["ebv_d"] = nc.dram_tensor("ebv_d", [H, 1536], F32, kind="Internal").ap()

    p.sem("ld")

    idb = p.sb("idb", [128, 128], BF16)
    cst = p.sb("cstb", [128, 4], F32)
    dog = p.sb("dogb", [H, 8], F32)
    sinks = p.sb("sinksb", [H, 1], F32)
    skexp = p.sb("skexp", [H, 1], F32)
    idf = p.sb("idf", [128, 128], F32)

    p.dma("sync", idb[:], g["id_bf"], inc="ld")
    p.dma("sync", idf[:], g["id_f32"], inc="ld")
    for ci in range(4):
        p.dma("sync", cst[:, ci:ci + 1],
              bass.AP(g["consts"].tensor, ci, [[0, 128], [1, 1]]), inc="ld")
    p.dma("sync", dog[:, 0:7], g["dog"], inc="ld")
    p.dma("sync", sinks[:], g["sinks"], inc="ld")
    c_ld0 = p.counts["ld"]

    p.sem("eb")
    _p0scope = p.scope()
    _p0scope.__enter__()
    ebv = p.sb("ebv", [H, 1536], F32)
    dist = p.sb("dist", [H, 1536], F32)
    disti = p.sb("disti", [H, 1536], I32)
    tc = p.sb("tmpc", [H, 8], F32)
    g1 = p.sb("g1b", [H, 1536], F32)

    p.wait("gpsimd", "ld", c_ld0)
    p.op("gpsimd", lambda e: e.iota(disti[:, :], pattern=[[1, 1536]], base=-512,
                                    channel_multiplier=0), inc="eb")
    p.wait("vector", "eb", p.counts["eb"])
    p.op("vector", lambda e: e.tensor_copy(dist[:, :], disti[:, :]), inc="eb")
    c_dist = p.counts["eb"]

    p.wait("scalar", "ld", c_ld0)
    def sp_col(src, dst):
        p.op("scalar", lambda e: e.activation(tc[:, 4:5], dog[:, src:src + 1], AF.Abs))
        p.op("scalar", lambda e: e.activation(tc[:, 4:5], tc[:, 4:5], AF.Exp, scale=-1.0))
        p.op("scalar", lambda e: e.activation(tc[:, 4:5], tc[:, 4:5], AF.Ln, bias=1.0))
        p.op("scalar", lambda e: e.activation(tc[:, 5:6], dog[:, src:src + 1], AF.Relu))
        p.op("scalar", lambda e: e.activation(tc[:, dst:dst + 1], tc[:, 5:6], AF.Identity, bias=tc[:, 4:5]))

    def inv2sq_col(src, dst):
        p.op("scalar", lambda e: e.activation(tc[:, 4:5], tc[:, src:src + 1], AF.Square))
        p.op("scalar", lambda e: e.activation(tc[:, 4:5], tc[:, 4:5], AF.Ln, scale=2.0))
        p.op("scalar", lambda e: e.activation(tc[:, 4:5], tc[:, 4:5], AF.Exp, scale=-1.0))
        p.op("scalar", lambda e: e.activation(tc[:, dst:dst + 1], tc[:, 4:5], AF.Identity, scale=-1.0))

    sp_col(1, 0)
    sp_col(4, 1)
    inv2sq_col(0, 2)
    inv2sq_col(1, 3)
    p.wait("scalar", "eb", c_dist)
    p.op("scalar", lambda e: e.activation(g1[:, :], dist[:, :], AF.Square, bias=dog[:, 2:3], scale=-1.0))
    p.op("scalar", lambda e: e.activation(g1[:, :], g1[:, :], AF.Exp, scale=tc[:, 2:3]))
    p.op("scalar", lambda e: e.activation(ebv[:, :], dist[:, :], AF.Square, bias=dog[:, 5:6], scale=-1.0))
    p.op("scalar", lambda e: e.activation(ebv[:, :], ebv[:, :], AF.Exp, scale=tc[:, 3:4]), inc="eb")
    c_g = p.counts["eb"]
    p.wait("vector", "eb", c_g)
    p.op("vector", lambda e: e.tensor_scalar(g1[:, :], g1[:, :], dog[:, 0:1], None, ALU.mult))
    p.op("vector", lambda e: e.scalar_tensor_tensor(ebv[:, :], ebv[:, :], dog[:, 3:4], g1[:, :], ALU.mult, ALU.subtract))
    p.op("vector", lambda e: e.tensor_scalar(ebv[:, :], ebv[:, :], -1.0, dog[:, 6:7], ALU.mult, ALU.add), inc="eb")
    c_b = p.counts["eb"]
    p.wait("scalar", "eb", c_b)
    p.op("scalar", lambda e: e.activation(ebv[:, :], ebv[:, :], AF.Exp), inc="eb")
    p.op("scalar", lambda e: e.activation(skexp[:, :], sinks[:, :], AF.Exp, bias=cst[0:H, 0:1]), inc="eb")
    c_b = p.counts["eb"]
    p.wait("vector", "eb", c_b)
    p.op("vector", lambda e: e.memset(ebv[:, 0:512], 0.0))
    p.op("vector", lambda e: e.memset(ebv[:, 1024:1536], 0.0), inc="eb")
    c_b = p.counts["eb"]
    p.wait("sync", "eb", c_b)
    p.dma("sync", g["ebv_d"], ebv[:, :], inc="eb")

    p.sem("pc")

    def precast(dst, src, rows, cols, chunk=512):
        for r0 in range(0, rows, chunk):
            r1 = min(rows, r0 + chunk)
            p.dma("gpsimd", dst[r0:r1, :], src[r0:r1, :], inc="pc")

    for nm in ("Wq", "Wk", "Wv", "Wo"):
        g[nm + "_bf"] = nc.dram_tensor(nm + "_bf", [DM, DM], BF16, kind="Internal").ap()
        precast(g[nm + "_bf"], g[nm], DM, DM)
    g["gateT_bf"] = nc.dram_tensor("gateT_bf", [DM, E], BF16, kind="Internal").ap()
    precast(g["gateT_bf"], g["gateT"], DM, E)
    for nm, rr, cc in (("w1T", DM, INTER), ("w2T", INTER, DM), ("w3T", DM, INTER)):
        g[nm + "_bf"] = nc.dram_tensor(nm + "_bf", [E, rr, cc], BF16, kind="Internal").ap()
        for e in range(E):
            precast(g[nm + "_bf"][e], g[nm][e], rr, cc)
    for nm, rr, cc in (("ws1T", DM, INTER), ("ws2T", INTER, DM), ("ws3T", DM, INTER)):
        g[nm + "_bf"] = nc.dram_tensor(nm + "_bf", [rr, cc], BF16, kind="Internal").ap()
        precast(g[nm + "_bf"], g[nm], rr, cc)
    for nm, rr, cc2 in (("inpT", DM, 2 * DI), ("outpT", DI, DM),
                        ("ffpT", DM, HID), ("ffoT", DI, DM)):
        g[nm + "_bf"] = nc.dram_tensor(nm + "_bf", [rr, cc2], BF16, kind="Internal").ap()
        precast(g[nm + "_bf"], g[nm], rr, cc2)
    c_pc_l0 = p.counts["pc"]

    g["ebp_d"] = nc.dram_tensor("ebp_d", [H, 2560], BF16, kind="Internal").ap()
    ebp = p.sb("ebp", [H, 2560], BF16)
    c_eb_now = p.counts["eb"]
    p.wait("vector", "eb", c_eb_now)
    p.seq("vector", lambda e: e.memset(ebp[:, :], 0.0))
    p.op("vector", lambda e: e.tensor_copy(ebp[:, 512:2048], ebv[:, :]), inc="eb")
    p.wait("sync", "eb", p.counts["eb"])
    p.dma("sync", g["ebp_d"], ebp[:, :], inc="eb")
    p.barrier()
    _p0scope.__exit__(None, None, None)


    g["xnT_d"] = nc.dram_tensor("xnT_d", [DM, TL], BF16, kind="Internal").ap()
    g["xnTlo_d"] = nc.dram_tensor("xnTlo_d", [DM, TL], BF16, kind="Internal").ap()
    g["xnTh_d"] = nc.dram_tensor("xnTh_d", [DM, WIN], BF16, kind="Internal").ap()
    p.stage("A:ln")
    with p.scope():
        emit_ln_T_to_D(p, g["x"], TL, g["xnT_d"], idb, cst, "l0a", lo_dram=g["xnTlo_d"])
        emit_ln_T_to_D(p, g["xhalo"], WIN, g["xnTh_d"], idb, cst, "l0b")
    p.barrier()

    if cfg.get("debug_out") == "xnT":
        dbg = nc.dram_tensor("dbg", [DM, TL + WIN], F32, kind="ExternalOutput").ap()
        p.sem("dbs")
        sf = p.sb("dbgsf", [128, TL + WIN], F32)
        sb2 = p.sb("dbgs2", [128, TL + WIN], BF16)
        for k in range(16):
            if k > 0:
                p.wait("sync", "dbs", p.counts["dbs"])
            p.dma("sync", sb2[:, 0:TL], g["xnT_d"][k * 128:(k + 1) * 128, :], inc="dbs")
            p.dma("sync", sb2[:, TL:], g["xnTh_d"][k * 128:(k + 1) * 128, :], inc="dbs")
            p.wait("vector", "dbs", p.counts["dbs"])
            cv = p.op("vector", lambda e: e.tensor_copy(sf[:, :], sb2[:, :]), inc="dbs")
            p.wait("sync", "dbs", cv)
            p.dma("sync", dbg[k * 128:(k + 1) * 128, :], sf[:, :], inc="dbs")
        p.wait("sync", "dbs", p.counts["dbs"])
        p.finalize()
        return nc, g

    for snm in ("lw", "lx", "ev", "sq", "sm", "sx", "sp", "sa", "se", "sd", "s2", "wo", "ac"):
        p.sem(snm)
    if cfg.get("grab"):
        p.grab_target = cfg["grab"]
        p.grab_dram = nc.dram_tensor("dbg2", [128, 12288], F32, kind="ExternalOutput").ap()
    p.stage("A:attn")
    emit_attention_l0(p, g, cst, idb, skexp)
    p.barrier()

    if cfg.get("debug_out") == "attn":
        dbg = nc.dram_tensor("dbg", [DM, TL], F32, kind="ExternalOutput").ap()
        p.dma("sync", dbg, g["acc0"], inc="ld")
        p.wait("sync", "ld", p.counts["ld"])
        p.finalize()
        return nc, g

    p.stage("A:moe")
    emit_moe_l0(p, g, cst, idf)
    p.barrier()

    if cfg.get("debug_out") == "l0":
        dbg = nc.dram_tensor("dbg", [DM, TL], F32, kind="ExternalOutput").ap()
        p.dma("sync", dbg, g["acc0"], inc="ld")
        p.wait("sync", "ld", p.counts["ld"])
        p.finalize()
        return nc, g

    TT = 16384

    def dout(name, shape, dt=F32):
        g[name] = nc.dram_tensor(name, shape, dt, kind="ExternalOutput").ap()

    dout("xn1o", [DM, TL], F32)
    dout("xn1bo", [DM, TL], BF16)
    dout("x1o", [TL, DM], F32)
    dout("xmo", [DI, TL], BF16)
    dout("zsilo", [DI, TL], BF16)
    dout("gamo", [1, 16], F32)
    g["acc1"] = g["xn1o"]
    g["xn1T_d"] = g["xn1bo"]
    g["ag1in"] = g["xmo"]
    g["zsil_d"] = g["zsilo"]
    with p.scope():
        u_res = p.sb("u_res", [128, 16 * 2048], BF16)
        p.stage("A:prelude")
        emit_l1_prelude(p, g, cst, idf, idb, u_res)
        if cfg.get("debug_out") != "pre":
            p.stage("A:inproj")
            emit_inproj(p, g, u_res)
    if cfg.get("debug_out") in ("pre", "inp"):
        p.finalize()
        return nc, g
    p.stage("A:quant1")
    emit_quant_pass1(p, g)
    p.finalize()
    return nc, g


def build_B1(cfg=None):
    nc = bass.Bass("TRN2", target_bir_lowering=False, debug=False)
    p = Prog(nc)
    g = {}
    TT = 16384

    def din(name, shape, dt=F32):
        g[name] = nc.dram_tensor(name, shape, dt, kind="ExternalInput").ap()

    din("xmch", [CHPC, TT], BF16)
    din("convw_s", [CHPC, DCONV]); din("convb_s", [CHPC, 1])
    din("xpwT_s", [CHPC, 160])
    g["xco"] = nc.dram_tensor("xco", [CHPC, TT], BF16, kind="ExternalOutput").ap()
    g["xdpo"] = nc.dram_tensor("xdpo", [160, TT], F32, kind="ExternalOutput").ap()
    g["xc_d"] = [nc.dram_tensor(f"xc_d{j}", [128, TT], BF16, kind="Internal").ap() for j in range(4)]
    for snm in ("sl", "sm2", "sv2", "sc2", "se2"):
        p.sem(snm)
    convw = p.sb("convwb", [128, 16], F32)
    convb = p.sb("convbb", [128, 4], F32)
    xpw = p.sb("xpwb", [128, 4 * 160], BF16)
    p.dma("sync", convw[:, :].rearrange("q (d c) -> q d c", c=4),
          g["convw_s"].rearrange("(d q) c -> q d c", q=128), inc="sl")
    p.dma("sync", convb[:, :].rearrange("q (d c) -> q d c", c=1),
          g["convb_s"].rearrange("(d q) c -> q d c", q=128), inc="sl",
          allow_slow_non_contiguous=True)
    p.dma("gpsimd", xpw[:, :].rearrange("q (d c) -> q d c", c=160),
          g["xpwT_s"].rearrange("(d q) c -> q d c", q=128), inc="sl")
    c_const = p.counts["sl"]
    p.wait("vector", "sl", c_const)
    p.wait("tensor", "sl", c_const)

    with p.scope():
        xmt = p.sb("xmt", [128, TT + 8], BF16)
        xct = p.sb("xct", [128, TT], F32)
        xcb = p.sb("xcbb", [128, TT], BF16)
        prev_cv = None
        prev_cs = None
        for dt_i in range(4):
            if dt_i > 0:
                p.wait("sync", "sl", p.counts["sl"])
                p.wait("sync", "sv2", prev_cv)
                p.wait("vector", "sc2", prev_cs)
            for seg in range(2):
                p.dma("sync", xmt[:, 4 + seg * 8196:4 + seg * 8196 + 8192],
                      g["xmch"][128 * dt_i:128 * (dt_i + 1), 8192 * seg:8192 * (seg + 1)], inc="sl")
            cg = p.counts["sl"]
            p.wait("vector", "sl", cg)
            p.seq("vector", lambda e: e.memset(xmt[:, 0:4], 0.0))
            p.seq("vector", lambda e: e.memset(xmt[:, 4 + 8192:8 + 8192], 0.0))
            for seg in range(2):
                ss = 4 + seg * (8192 + 4)
                od = 8192 * seg
                p.seq("vector", lambda e, ss=ss, od=od, dt_i=dt_i: e.tensor_scalar(
                    xct[:, od:od + 8192], xmt[:, ss - 3:ss - 3 + 8192],
                    convw[:, 4 * dt_i:4 * dt_i + 1], None, ALU.mult))
                for tap in range(1, 4):
                    p.seq("vector", lambda e, ss=ss, od=od, tap=tap, dt_i=dt_i: e.scalar_tensor_tensor(
                        xct[:, od:od + 8192], xmt[:, ss - 3 + tap:ss - 3 + tap + 8192],
                        convw[:, 4 * dt_i + tap:4 * dt_i + tap + 1],
                        xct[:, od:od + 8192], ALU.mult, ALU.add))
            cv = p.op("vector", lambda e: e.engine_nop(), inc="sv2")
            prev_cv = cv
            p.wait("scalar", "sv2", cv)
            for half in range(4):
                p.seq("scalar", lambda e, half=half, dt_i=dt_i: e.activation(
                    xcb[:, 4096 * half:4096 * (half + 1)], xct[:, 4096 * half:4096 * (half + 1)],
                    AF.Silu, bias=convb[:, dt_i:dt_i + 1]))
            cs = p.op("scalar", lambda e: e.nop(), inc="sc2")
            prev_cs = cs
            p.wait("sync", "sc2", cs)
            p.dma("sync", g["xc_d"][dt_i], xcb[:, :], inc="sl")
            p.dma("sync", g["xco"][128 * dt_i:128 * (dt_i + 1), :], xcb[:, :], inc="sl")
        p.wait("sync", "sl", p.counts["sl"])
    p.barrier()

    with p.scope():
        xcc = [p.sb(f"xcc{j}", [128, 512], BF16) for j in range(8)]
        ps1 = p.ps("ps_xd1", [128, 512], F32)
        ps2 = p.ps("ps_xd2", [32, 512], F32)
        st1 = p.sb("xdst1", [128, 512], F32)
        st2 = p.sb("xdst2", [32, 512], F32)
        dmad = {}
        for ncj in range(32):
            for dt_i in range(4):
                sl_ = xcc[(ncj * 4 + dt_i) % 8]
                prev = dmad.get(("mm", (ncj * 4 + dt_i) % 8))
                if prev is not None:
                    p.wait("sync", "sm2", prev)
                p.dma("sync", sl_[:, :], g["xc_d"][dt_i][:, 512 * ncj:512 * (ncj + 1)], inc="sl")
                cl = p.counts["sl"]
                p.wait("tensor", "sl", cl)
                if dt_i == 0:
                    p.wait("tensor", "se2", p.counts["se2"])
                p.op("tensor", lambda e, sl_=sl_, dt_i=dt_i: e.matmul(
                    ps1[:, :], xpw[:, 160 * dt_i:160 * dt_i + 128], sl_[:, :],
                    start=(dt_i == 0), stop=(dt_i == 3)), inc=None)
                dmad[("mm", (ncj * 4 + dt_i) % 8)] = p.op(
                    "tensor", lambda e, sl_=sl_, dt_i=dt_i: e.matmul(
                        ps2[:, :], xpw[:, 160 * dt_i + 128:160 * dt_i + 160], sl_[:, :],
                        start=(dt_i == 0), stop=(dt_i == 3)),
                    inc="sm2")
            cm = p.counts["sm2"]
            p.wait("vector", "sm2", cm)
            p.op("vector", lambda e: e.tensor_copy(st1[:, :], ps1[:, :]))
            cv = p.op("vector", lambda e: e.tensor_copy(st2[:, :], ps2[:, :]), inc="se2")
            p.wait("sync", "se2", cv)
            p.dma("sync", g["xdpo"][0:128, 512 * ncj:512 * (ncj + 1)], st1[:, :], inc="sl")
            p.dma("sync", g["xdpo"][128:160, 512 * ncj:512 * (ncj + 1)], st2[:, :], inc="sl")
        p.wait("sync", "sl", p.counts["sl"])
        p.wait("gpsimd", "sl", p.counts["sl"])
    p.finalize()
    return nc, g


def build_B2(cfg=None):
    nc = bass.Bass("TRN2", target_bir_lowering=False, debug=False)
    p = Prog(nc)
    g = {}
    TT = 16384

    def din(name, shape, dt=F32):
        g[name] = nc.dram_tensor(name, shape, dt, kind="ExternalInput").ap()

    din("xdai", [8 * 160, TT])
    din("xci", [CHPC, TT], BF16)
    din("dtwT_s", [DTRANK, CHPC])
    din("dtb_s", [CHPC, 1]); din("alog_s", [CHPC, DS]); din("dskip_s", [CHPC, 1])
    din("gami", [1, 128])
    din("bw1T_s", [DMPC, BH]); din("bw2T_s", [DMPC, BH]); din("bw3T_s", [BHPC, DM])
    din("consts", [4])
    g["yo"] = nc.dram_tensor("yo", [CHPC, TT], BF16, kind="ExternalOutput").ap()
    g["q1o"] = nc.dram_tensor("q1o", [DMPC, BH], BF16, kind="ExternalOutput").ap()
    g["q2o"] = nc.dram_tensor("q2o", [DMPC, BH], BF16, kind="ExternalOutput").ap()
    g["q3o"] = nc.dram_tensor("q3o", [BHPC, DM], BF16, kind="ExternalOutput").ap()
    g["xdbl_d"] = nc.dram_tensor("xdbl_d", [160, TT], F32, kind="Internal").ap()
    g["xdblb_d"] = nc.dram_tensor("xdblb_d", [32, TT], BF16, kind="Internal").ap()
    g["xc_d"] = [g["xci"][128 * j:128 * (j + 1), :] for j in range(4)]
    for snm in ("ld", "sl", "sm2", "sv2", "sc2", "se2", "sg2"):
        p.sem(snm)
    cst = p.sb("cstb", [128, 4], F32)
    for ci in range(4):
        p.dma("sync", cst[:, ci:ci + 1],
              bass.AP(g["consts"].tensor, ci, [[0, 128], [1, 1]]), inc="ld")

    with p.scope():
        xa = [p.sb(f"xab{j}", [128, 2048], F32) for j in range(2)]
        xs_ = p.sb("xsumb", [128, 2048], F32)
        xsb = p.sb("xsumbb", [32, 2048], BF16)
        dnn = {}
        for rg0, rgn in ((0, 128), (128, 32)):
            for ncj in range(8):
                for r in range(8):
                    sl_ = xa[r % 2]
                    if dnn.get(r % 2) is not None:
                        p.wait("sync", "sv2", dnn[r % 2])
                    cl = p.dma("sync", sl_[0:rgn, :],
                               g["xdai"][160 * r + rg0:160 * r + rg0 + rgn,
                                         2048 * ncj:2048 * (ncj + 1)], inc="sl")
                    p.wait("vector", "sl", cl)
                    if r == 0:
                        dnn[r % 2] = p.op("vector", lambda e, sl_=sl_, rgn=rgn: e.tensor_copy(
                            xs_[0:rgn, :], sl_[0:rgn, :]), inc="sv2")
                    else:
                        dnn[r % 2] = p.op("vector", lambda e, sl_=sl_, rgn=rgn: e.tensor_add(
                            xs_[0:rgn, :], xs_[0:rgn, :], sl_[0:rgn, :]), inc="sv2")
                if rg0 == 128:
                    p.op("vector", lambda e, rgn=rgn: e.tensor_copy(
                        xsb[0:rgn, :], xs_[0:rgn, :]), inc="sv2")
                cv = p.counts["sv2"]
                p.wait("sync", "sv2", cv)
                cd = p.dma("sync", g["xdbl_d"][rg0:rg0 + rgn, 2048 * ncj:2048 * (ncj + 1)],
                           xs_[0:rgn, :], inc="sl")
                if rg0 == 128:
                    cd = p.dma("sync", g["xdblb_d"][:, 2048 * ncj:2048 * (ncj + 1)],
                               xsb[0:rgn, :], inc="sl")
                p.wait("vector", "sl", cd)
    p.barrier()

    p.stage("B2:scan")
    emit_scan_core(p, g, cst)
    p.stage("B2:quant2")
    emit_quant_pass2(p, g, cst)
    p.finalize()
    return nc, g


def build_C(cfg=None):
    nc = bass.Bass("TRN2", target_bir_lowering=False, debug=False)
    p = Prog(nc)
    g = {}
    cfg = cfg or {}

    def din(name, shape, dt=F32):
        g[name] = nc.dram_tensor(name, shape, dt, kind="ExternalInput").ap()

    din("ymy", [DI, TL], BF16)
    din("zsili", [DI, TL], BF16)
    din("xn1i", [DM, TL], F32)
    din("x1i", [TL, DM], F32)
    din("xn1bi", [DM, TL], BF16)
    din("q1i", [DM, BH], BF16); din("q2i", [DM, BH], BF16); din("q3i", [BH, DM], BF16)
    din("outpT", [DI, DM]); din("ffpT", [DM, HID]); din("ffoT", [DI, DM])
    din("ln2w", [DM, 1]); din("ln2b", [DM, 1])
    din("id_bf", [128, 128], BF16); din("id_f32", [128, 128]); din("consts", [4])
    g["out"] = nc.dram_tensor("out", [TL, DM], F32, kind="ExternalOutput").ap()
    g["acc1"] = nc.dram_tensor("acc1", [DM, TL], F32, kind="Internal").ap()
    g["ydz_d"] = nc.dram_tensor("ydz_d", [DI, TL], BF16, kind="Internal").ap()
    g["a_d"] = nc.dram_tensor("a_d", [DI, TL], BF16, kind="Internal").ap()
    g["sgate_d"] = nc.dram_tensor("sgate_d", [DI, TL], BF16, kind="Internal").ap()
    g["g2_d"] = nc.dram_tensor("g2_d", [DI, TL], BF16, kind="Internal").ap()
    g["sg1_d"] = nc.dram_tensor("sg1_d", [BH, TL], BF16, kind="Internal").ap()
    g["x2_d"] = nc.dram_tensor("x2_d", [BH, TL], BF16, kind="Internal").ap()
    g["hid_d"] = nc.dram_tensor("hid_d", [BH, TL], BF16, kind="Internal").ap()
    g["xn1T_d"] = g["xn1bi"]
    for snm in ("ld", "pc", "yd", "g2", "bh"):
        p.sem(snm)
    idb = p.sb("idb", [128, 128], BF16)
    idf = p.sb("idf", [128, 128], F32)
    cst = p.sb("cstb", [128, 4], F32)
    p.dma("sync", idb[:], g["id_bf"], inc="ld")
    p.dma("sync", idf[:], g["id_f32"], inc="ld")
    for ci in range(4):
        p.dma("sync", cst[:, ci:ci + 1],
              bass.AP(g["consts"].tensor, ci, [[0, 128], [1, 1]]), inc="ld")

    def precast(dst, src, rows, cols, chunk=512):
        for r0 in range(0, rows, chunk):
            r1 = min(rows, r0 + chunk)
            p.dma("gpsimd", dst[r0:r1, :], src[r0:r1, :], inc="pc")

    for nm, rr, cc2 in (("outpT", DI, DM), ("ffpT", DM, HID), ("ffoT", DI, DM)):
        g[nm + "_bf"] = nc.dram_tensor(nm + "_bf", [rr, cc2], BF16, kind="Internal").ap()
        precast(g[nm + "_bf"], g[nm], rr, cc2)
    p.dma("sync", g["acc1"], g["xn1i"], inc="ld")
    p.wait("sync", "ld", p.counts["ld"])
    p.wait("gpsimd", "pc", p.counts["pc"])
    p.barrier()

    p.stage("C:ydz")
    acc_last = {}
    with p.scope():
        yg = [p.sb(f"ygb{j}", [128, TL], BF16) for j in range(2)]
        zg = [p.sb(f"zgb{j}", [128, TL], BF16) for j in range(2)]
        od = ["ydo0", "ydo1"]
        for s_ in od:
            p.sem(s_)
        prevv = {}
        for kt in range(32):
            if prevv.get(kt % 2) is not None:
                p.wait("sync", od[kt % 2], prevv[kt % 2])
            p.dma("sync", yg[kt % 2][:, :], g["ymy"][128 * kt:128 * (kt + 1), :], inc="yd")
            cl = p.dma("sync", zg[kt % 2][:, :], g["zsili"][128 * kt:128 * (kt + 1), :], inc="yd")
            p.wait("vector", "yd", cl)
            cv = p.op("vector", lambda e, kt=kt: e.tensor_mul(
                yg[kt % 2][:, :], yg[kt % 2][:, :], zg[kt % 2][:, :]), inc="yd")
            p.wait("sync", "yd", cv)
            prevv[kt % 2] = p.dma("sync", g["ydz_d"][128 * kt:128 * (kt + 1), :],
                                  yg[kt % 2][:, :], inc=od[kt % 2])
        p.wait("sync", od[0], p.counts[od[0]])
        p.wait("sync", od[1], p.counts[od[1]])
    p.barrier()
    p.stage("C:op")
    emit_contract_acc(p, g, g["outpT_bf"], g["ydz_d"], 32, g["acc1"], acc_last, "op")
    p.barrier()

    if cfg.get("debug_out") == "mix":
        dbg = nc.dram_tensor("dbg", [DM, TL], F32, kind="ExternalOutput").ap()
        p.dma("sync", dbg, g["acc1"], inc="ld")
        p.wait("sync", "ld", p.counts["ld"])
        p.finalize()
        return nc, g

    p.stage("C:ln2")
    with p.scope():
        v2T = p.sb("v2T_res", [128, 16 * 2048], BF16)
        emit_ln2_v2T(p, g, cst, idf, idb, v2T)
        if cfg.get("debug_out") == "ln2":
            dbg = nc.dram_tensor("dbg", [DM, TL], F32, kind="ExternalOutput").ap()
            _sf = p.sb("dbgsf", [128, TL], F32)
            p.sem("dbs")
            for k in range(16):
                if k > 0:
                    p.wait("vector", "dbs", p.counts["dbs"])
                cv = p.op("vector", lambda e, k=k: e.tensor_copy(_sf[:, :], v2T[:, 2048 * k:2048 * (k + 1)]), inc="dbs")
                p.wait("sync", "dbs", cv)
                p.dma("sync", dbg[128 * k:128 * (k + 1), :], _sf[:, :], inc="dbs")
            p.wait("sync", "dbs", p.counts["dbs"])

        def ff_writer(mo, nj, ps, sl, ei, ai):
            if mo < 32:
                ec = p.op("scalar", lambda e, ps=ps, sl=sl: e.copy(sl[:, :], ps[:, :]), inc=ei)
                p.wait("sync", ei, ec)
                ac = p.dma("sync", g["a_d"][128 * mo:128 * (mo + 1), 512 * nj:512 * (nj + 1)],
                           sl[:, :], inc=ai)
            else:
                ec = p.op("scalar", lambda e, ps=ps, sl=sl: e.activation(sl[:, :], ps[:, :], AF.Silu), inc=ei)
                p.wait("sync", ei, ec)
                ac = p.dma("sync", g["sgate_d"][128 * (mo - 32):128 * (mo - 31), 512 * nj:512 * (nj + 1)],
                           sl[:, :], inc=ai)
            return ec, ac
        if cfg.get("debug_out") != "ln2":
            p.stage("C:ffp")
            emit_proj_big(p, g, g["ffpT_bf"], v2T, 64, ff_writer, "fp")
    if cfg.get("debug_out") == "ln2":
        p.finalize()
        return nc, g
    p.barrier()
    if cfg.get("debug_out") == "ffp":
        dbg = nc.dram_tensor("dbg", [DM, TL], F32, kind="ExternalOutput").ap()
        p.dma("sync", dbg, g["acc1"], inc="ld")
        p.wait("sync", "ld", p.counts["ld"])
        p.finalize()
        return nc, g
    p.stage("C:g2")
    with p.scope():
        ag_ = [p.sb(f"g2a{j}", [128, TL], BF16) for j in range(2)]
        sg_ = [p.sb(f"g2s{j}", [128, TL], BF16) for j in range(2)]
        od = ["g2o0", "g2o1"]
        for s_ in od:
            p.sem(s_)
        prevv = {}
        for kt in range(32):
            if prevv.get(kt % 2) is not None:
                p.wait("sync", od[kt % 2], prevv[kt % 2])
            p.dma("sync", ag_[kt % 2][:, :], g["a_d"][128 * kt:128 * (kt + 1), :], inc="g2")
            cl = p.dma("sync", sg_[kt % 2][:, :], g["sgate_d"][128 * kt:128 * (kt + 1), :], inc="g2")
            p.wait("vector", "g2", cl)
            cv = p.op("vector", lambda e, kt=kt: e.tensor_mul(
                ag_[kt % 2][:, :], ag_[kt % 2][:, :], sg_[kt % 2][:, :]), inc="g2")
            p.wait("sync", "g2", cv)
            prevv[kt % 2] = p.dma("sync", g["g2_d"][128 * kt:128 * (kt + 1), :],
                                  ag_[kt % 2][:, :], inc=od[kt % 2])
        p.wait("sync", od[0], p.counts[od[0]])
        p.wait("sync", od[1], p.counts[od[1]])
    p.barrier()
    if cfg.get("debug_out") == "g2s":
        dbg = nc.dram_tensor("dbg", [3 * DI, TL], BF16, kind="ExternalOutput").ap()
        p.dma("sync", dbg[0:DI, :], g["a_d"], inc="ld")
        p.dma("sync", dbg[DI:2 * DI, :], g["sgate_d"], inc="ld")
        p.dma("sync", dbg[2 * DI:3 * DI, :], g["g2_d"], inc="ld")
        p.wait("sync", "ld", p.counts["ld"])
        p.finalize()
        return nc, g
    p.stage("C:fo")
    emit_contract_acc(p, g, g["ffoT_bf"], g["g2_d"], 32, g["acc1"], acc_last, "fo")
    p.barrier()
    if cfg.get("debug_out") == "blfull":
        g["dbg4"] = nc.dram_tensor("dbg4", [DM, TL], F32, kind="ExternalOutput").ap()
        p.dma("sync", g["dbg4"], g["acc1"], inc="ld")
        p.wait("sync", "ld", p.counts["ld"])
        p.barrier()

    if cfg.get("debug_out") == "ff":
        dbg = nc.dram_tensor("dbg", [DM, TL], F32, kind="ExternalOutput").ap()
        p.dma("sync", dbg, g["acc1"], inc="ld")
        p.wait("sync", "ld", p.counts["ld"])
        p.finalize()
        return nc, g

    p.stage("C:b12")
    with p.scope():
        xnf1 = p.sb("xnf1_res", [128, 16 * 2048], BF16)
        p.sem("bx")
        for kk in range(16):
            p.dma("sync", xnf1[:, 2048 * kk:2048 * (kk + 1)],
                  g["xn1T_d"][kk * 128:(kk + 1) * 128, :], inc="bx")
        p.wait("tensor", "bx", p.counts["bx"])
        p.wait("scalar", "bx", p.counts["bx"])

        def b1_writer(mo, nj, ps, sl, ei, ai):
            ec = p.op("scalar", lambda e, ps=ps, sl=sl: e.activation(sl[:, :], ps[:, :], AF.Sigmoid), inc=ei)
            p.wait("sync", ei, ec)
            ac = p.dma("sync", g["sg1_d"][128 * mo:128 * (mo + 1), 512 * nj:512 * (nj + 1)],
                       sl[:, :], inc=ai)
            return ec, ac

        def b2_writer(mo, nj, ps, sl, ei, ai):
            ec = p.op("scalar", lambda e, ps=ps, sl=sl: e.copy(sl[:, :], ps[:, :]), inc=ei)
            p.wait("sync", ei, ec)
            ac = p.dma("sync", g["x2_d"][128 * mo:128 * (mo + 1), 512 * nj:512 * (nj + 1)],
                       sl[:, :], inc=ai)
            return ec, ac
        emit_proj_big(p, g, g["q1i"], xnf1, 44, b1_writer, "b1")
        p.barrier()
        emit_proj_big(p, g, g["q2i"], xnf1, 44, b2_writer, "b2")
    p.barrier()
    p.stage("C:hid")
    with p.scope():
        h1 = [p.sb(f"bh1{j}", [128, TL], BF16) for j in range(2)]
        h2 = [p.sb(f"bh2{j}", [128, TL], BF16) for j in range(2)]
        od = ["bho0", "bho1"]
        for s_ in od:
            p.sem(s_)
        prevv = {}
        for kt in range(44):
            if prevv.get(kt % 2) is not None:
                p.wait("sync", od[kt % 2], prevv[kt % 2])
            p.dma("sync", h1[kt % 2][:, :], g["sg1_d"][128 * kt:128 * (kt + 1), :], inc="bh")
            cl = p.dma("sync", h2[kt % 2][:, :], g["x2_d"][128 * kt:128 * (kt + 1), :], inc="bh")
            p.wait("vector", "bh", cl)
            cv = p.op("vector", lambda e, kt=kt: e.tensor_mul(
                h1[kt % 2][:, :], h1[kt % 2][:, :], h2[kt % 2][:, :]), inc="bh")
            p.wait("sync", "bh", cv)
            prevv[kt % 2] = p.dma("sync", g["hid_d"][128 * kt:128 * (kt + 1), :],
                                  h1[kt % 2][:, :], inc=od[kt % 2])
        p.wait("sync", od[0], p.counts[od[0]])
        p.wait("sync", od[1], p.counts[od[1]])
    p.barrier()
    if cfg.get("debug_out") == "bls":
        dbg = nc.dram_tensor("dbg", [3 * BH, TL], BF16, kind="ExternalOutput").ap()
        p.dma("sync", dbg[0:BH, :], g["sg1_d"], inc="ld")
        p.dma("sync", dbg[BH:2 * BH, :], g["x2_d"], inc="ld")
        p.dma("sync", dbg[2 * BH:3 * BH, :], g["hid_d"], inc="ld")
        p.wait("sync", "ld", p.counts["ld"])
        p.finalize()
        return nc, g
    p.stage("C:b3")
    emit_contract_acc(p, g, g["q3i"], g["hid_d"], 44, g["acc1"], acc_last, "b3")
    p.barrier()

    if cfg.get("debug_out") == "bl":
        dbg = nc.dram_tensor("dbg", [DM, TL], F32, kind="ExternalOutput").ap()
        p.dma("sync", dbg, g["acc1"], inc="ld")
        p.wait("sync", "ld", p.counts["ld"])
        p.finalize()
        return nc, g

    p.stage("C:fin")
    emit_finale(p, g, cst, idf)
    if cfg.get("debug_out") == "blfull":
        dbg = nc.dram_tensor("dbg", [DM, TL], F32, kind="ExternalOutput").ap()
        p.dma("sync", dbg, g["acc1"], inc="ld")
        dbg2 = nc.dram_tensor("dbg2", [3 * BH, TL], BF16, kind="ExternalOutput").ap()
        p.dma("sync", dbg2[0:BH, :], g["sg1_d"], inc="ld")
        p.dma("sync", dbg2[BH:2 * BH, :], g["x2_d"], inc="ld")
        p.dma("sync", dbg2[2 * BH:3 * BH, :], g["hid_d"], inc="ld")
        dbg3 = nc.dram_tensor("dbg3", [DI, TL], BF16, kind="ExternalOutput").ap()
        p.dma("sync", dbg3, g["g2_d"], inc="ld")
        p.wait("sync", "ld", p.counts["ld"])
    p.finalize()
    return nc, g


def emit_attention_l0(p, g, cst, idb, skexp):
    nc = p.nc
    g["dn_d"] = nc.dram_tensor("dn_d", [H * WIN], BF16, kind="Internal").ap()
    NW = 1536
    _ascope = p.scope()
    _ascope.__enter__()
    idrev = p.sb("idrev", [128, 128], BF16)
    m0b = p.sb("m0b", [128, 8 * 512], BF16)
    p.dma("sync", idrev[:], g["id_rev"], inc="lx")
    for kt in range(8):
        p.dma("gpsimd", m0b[:, 512 * kt:512 * (kt + 1)],
              g["mask0T"][128 * kt:128 * (kt + 1), :], inc="lx")
    c_m0 = p.counts["lx"]

    for sb in range(2):
        with p.scope():
            kT = p.sb("kT", [128, 16 * NW], BF16)
            vb = p.sb("vb", [128, 12 * 2080], BF16)
            qT = p.sb("qT", [128, 16 * 1024], BF16)
            p.op("vector", lambda e: e.memset(vb[:, :], 1.0), inc="ev")
            c_vb1 = p.counts["ev"]

            with p.scope():
                xw = p.sb("xw", [128, 16 * NW], BF16)
                wq4 = p.sb("wq4", [128, 16 * 512], BF16)
                for kk in range(16):
                    if sb == 0:
                        p.dma("sync", xw[:, NW * kk:NW * kk + WIN],
                              g["xnTh_d"][kk * 128:(kk + 1) * 128, :], inc="lx")
                        p.dma("sync", xw[:, NW * kk + WIN:NW * (kk + 1)],
                              g["xnT_d"][kk * 128:(kk + 1) * 128, 0:1024], inc="lx")
                    else:
                        p.dma("sync", xw[:, NW * kk:NW * (kk + 1)],
                              g["xnT_d"][kk * 128:(kk + 1) * 128, WIN:WIN + NW], inc="lx")
                c_xw = p.counts["lx"]
                ps_p = [p.ps(f"ps_p{j}", [128, 512], F32) for j in range(4)]

                def evac(eng, fn):
                    return p.op(eng, fn, inc="ev")

                def proj_wstat(wbf_d, dst, dstw, n_tok, src_off):
                    ti = 0
                    ev0 = p.counts["ev"]
                    for mq in range(4):
                        if True:
                            p.wait("sync", "sm", p.counts["sm"])
                        for kk in range(16):
                            p.dma("sync", wq4[:, 512 * kk:512 * (kk + 1)],
                                  wbf_d[kk * 128:(kk + 1) * 128, 512 * mq:512 * (mq + 1)], inc="lw")
                        cw = p.counts["lw"]
                        p.wait("tensor", "lw", cw)
                        if mq == 0:
                            p.wait("tensor", "lx", c_xw)
                        for m in range(4):
                            for nj in range(cdiv(n_tok, 512)):
                                nc_ = min(512, n_tok - nj * 512)
                                ps = ps_p[ti % 4]
                                if ti >= 4:
                                    p.wait("tensor", "ev", ev0 + ti - 3)
                                for kk in range(16):
                                    p.op("tensor", lambda e, kk=kk, m=m, nj=nj, ps=ps, nc_=nc_, so=src_off:
                                         e.matmul(ps[:, 0:nc_],
                                                  wq4[:, 512 * kk + 128 * m:512 * kk + 128 * (m + 1)],
                                                  xw[:, NW * kk + so + nj * 512:NW * kk + so + nj * 512 + nc_],
                                                  start=(kk == 0), stop=(kk == 15)),
                                         inc=("sm" if kk == 15 else None))
                                cm = p.counts["sm"]
                                mo = mq * 4 + m
                                eng = "scalar" if ti % 2 == 0 else "vector"
                                p.wait(eng, "sm", cm)
                                if eng == "scalar":
                                    evac(eng, lambda e, ps=ps, mo=mo, nj=nj, nc_=nc_, dst=dst, dstw=dstw:
                                         e.copy(dst[:, dstw * mo + nj * 512:dstw * mo + nj * 512 + nc_], ps[:, 0:nc_]))
                                else:
                                    evac(eng, lambda e, ps=ps, mo=mo, nj=nj, nc_=nc_, dst=dst, dstw=dstw:
                                         e.tensor_copy(dst[:, dstw * mo + nj * 512:dstw * mo + nj * 512 + nc_], ps[:, 0:nc_]))
                                ti += 1
                    p.wait("sync", "ev", p.counts["ev"])

                proj_wstat(g["Wk_bf"], kT, NW, NW, 0)
                proj_wstat(g["Wq_bf"], qT, 1024, 1024, WIN)

                ti = 0
                ev0 = p.counts["ev"]
                vbv = vb[:, :].rearrange("q (t h c) -> q t h c", t=12, h=32)
                for oq in range(4):
                    p.wait("sync", "sm", p.counts["sm"])
                    for kk in range(16):
                        p.dma("sync", wq4[:, 512 * kk:512 * (kk + 1)],
                              g["Wv_bf"][kk * 128:(kk + 1) * 128, 512 * oq:512 * (oq + 1)], inc="lw")
                    cw = p.counts["lw"]
                    p.wait("tensor", "lw", cw)
                    for tt in range(12):
                        ps = ps_p[ti % 4]
                        if ti >= 4:
                            p.wait("tensor", "ev", ev0 + ti - 3)
                        for kk in range(16):
                            p.op("tensor", lambda e, kk=kk, tt=tt, ps=ps:
                                 e.matmul(ps[:, :],
                                          xw[:, NW * kk + 128 * tt:NW * kk + 128 * (tt + 1)],
                                          wq4[:, 512 * kk:512 * (kk + 1)],
                                          start=(kk == 0), stop=(kk == 15)),
                                 inc=("sm" if kk == 15 else None))
                        cm = p.counts["sm"]
                        eng = "scalar" if ti % 2 == 0 else "vector"
                        p.wait(eng, "sm", cm)
                        p.wait(eng, "ev", c_vb1)
                        dv = vbv[:, tt, oq * 8:(oq + 1) * 8, 0:64]
                        sv = ps[:, :].rearrange("q (h c) -> q h c", c=64)
                        if eng == "scalar":
                            evac(eng, lambda e, dv=dv, sv=sv: e.copy(dv, sv))
                        else:
                            evac(eng, lambda e, dv=dv, sv=sv: e.tensor_copy(dv, sv))
                        ti += 1
                p.wait("sync", "ev", p.counts["ev"])
                p.wait("gpsimd", "ev", p.counts["ev"])

            with p.scope():
                ebst = p.sb("ebst", [128, 8 * 512], BF16)
                eb0 = p.sb("eb0", [128, 8 * 512], BF16)
                btoe = [p.sb(f"btoe{j}", [128, 512], BF16) for j in range(2)]
                ps_eb = [p.ps(f"ps_eb{j}", [128, 512], F32) for j in range(2)]
                pT = p.sb("pTb", [128, 8 * 512], BF16)
                avS = p.sb("avS", [128, 16 * 512], BF16)
                avF = avS
                dnm = p.sb("dnm", [H, 512], F32)
                dnmb = p.sb("dnmb", [H, 512], BF16)
                dnst = [p.sb(f"dnst{j}", [1, 512], F32) for j in range(2)]
                dnb = [p.sb(f"dnb{j}", [128, 512], BF16) for j in range(2)]
                ps_s = [p.ps(f"ps_s{j}", [128, 1024], F32) for j in range(2)]
                ps_av = p.ps("ps_av", [65, 512], F32)
                wo4 = p.sb("wo4", [128, 8 * 512], BF16)
                stg = [p.sb(f"wstg{j}", [128, 512], F32) for j in range(2)]

                for blk in range(2):
                    gblk = sb * 2 + blk
                    w0 = blk * WIN
                    for h in range(H):
                        idx = p.counts["sa"]
                        mchunk = h // 2
                        rbase = 64 * (h % 2)
                        if idx >= 1:
                            p.wait("scalar", "sp", 4 * idx)
                            p.wait("vector", "sp", 4 * idx)
                        ev_mm = {}
                        ev_ev = {}
                        for kt in range(8):
                            bt = btoe[kt % 2]
                            if kt >= 2:
                                p.wait("sync", "sq", ev_mm[kt - 2])
                            p.dma("sync", bt[:, :],
                                  bass.AP(g["ebp_d"].tensor, h * 2560 + 1409 - 128 * kt,
                                          [[1, 128], [1, 512]]), inc="lx")
                            cl = p.counts["lx"]
                            pe = ps_eb[kt % 2]
                            p.wait("tensor", "lx", cl)
                            if kt >= 2:
                                p.wait("tensor", "sq", ev_ev[kt - 2])
                            ev_mm[kt] = p.op("tensor", lambda e, bt=bt, pe=pe: e.matmul(
                                pe[:, :], idrev[:, :], bt[:, :], start=True, stop=True), inc="sq")
                            cm = p.counts["sq"]
                            eng = "scalar" if kt % 2 == 0 else "vector"
                            p.wait(eng, "sq", cm)
                            if eng == "scalar":
                                ev_ev[kt] = p.op(eng, lambda e, kt=kt, pe=pe: e.copy(
                                    ebst[:, 512 * kt:512 * (kt + 1)], pe[:, :]), inc="sq")
                            else:
                                ev_ev[kt] = p.op(eng, lambda e, kt=kt, pe=pe: e.tensor_copy(
                                    ebst[:, 512 * kt:512 * (kt + 1)], pe[:, :]), inc="sq")
                            if gblk == 0:
                                cq = p.counts["sq"]
                                p.wait("vector", "sq", cq)
                                if kt == 0:
                                    p.wait("vector", "lx", c_m0)
                                ev_ev[kt] = p.op("vector", lambda e, kt=kt: e.tensor_mul(
                                    eb0[:, 512 * kt:512 * (kt + 1)],
                                    ebst[:, 512 * kt:512 * (kt + 1)],
                                    m0b[:, 512 * kt:512 * (kt + 1)]), inc="sq")
                        ebuse = eb0 if gblk == 0 else ebst
                        c_ebdone = p.counts["sq"]

                        for qr in range(4):
                            qidx = 4 * idx + qr
                            ps = ps_s[qidx % 2]
                            if qidx >= 2:
                                p.wait("tensor", "sx", qidx - 1)
                            for k2 in range(2):
                                kt = qr * 2 + k2
                                p.op("tensor", lambda e, kt=kt, k2=k2, ps=ps, mchunk=mchunk, rbase=rbase, w0=w0, blk=blk:
                                     e.matmul(ps[:, 512 * k2:512 * (k2 + 1)],
                                              kT[rbase:rbase + 64, NW * mchunk + w0 + 128 * kt:NW * mchunk + w0 + 128 * (kt + 1)],
                                              qT[rbase:rbase + 64, 1024 * mchunk + 512 * blk:1024 * mchunk + 512 * (blk + 1)],
                                              start=True, stop=True),
                                     inc=("sm" if k2 == 1 else None))
                            cm = p.counts["sm"]
                            p.wait("scalar", "sm", cm)
                            if qr == 0:
                                p.wait("scalar", "sq", c_ebdone)
                            if idx >= 1 and qr == 0:
                                p.wait("scalar", "sa", idx)
                            p.op("scalar", lambda e, qr=qr, ps=ps: e.activation(
                                pT[:, 1024 * qr:1024 * (qr + 1)], ps[:, :], AF.Exp,
                                bias=cst[0:128, 0:1], scale=0.125), inc="sx")
                            cx = p.counts["sx"]
                            p.wait("vector", "sx", cx)
                            p.op("vector", lambda e, qr=qr, ebuse=ebuse: e.tensor_mul(
                                pT[:, 1024 * qr:1024 * (qr + 1)],
                                pT[:, 1024 * qr:1024 * (qr + 1)],
                                ebuse[:, 1024 * qr:1024 * (qr + 1)]), inc="sp")

                        p.wait("tensor", "se", 2 * idx)
                        for kt in range(8):
                            p.wait("tensor", "sp", 4 * idx + kt // 2 + 1)
                            wt = blk * 4 + kt
                            p.op("tensor", lambda e, kt=kt, wt=wt, h=h:
                                 e.matmul(ps_av[:, :],
                                          vb[:, 2080 * wt + 65 * h:2080 * wt + 65 * (h + 1)],
                                          pT[:, 512 * kt:512 * (kt + 1)],
                                          start=(kt == 0), stop=(kt == 7)),
                                 inc=("sa" if kt == 7 else None))
                        ca = p.counts["sa"]
                        p.wait("scalar", "sa", ca)
                        p.op("scalar", lambda e, mchunk=mchunk, rbase=rbase: e.copy(
                            avS[rbase:rbase + 64, 512 * mchunk:512 * (mchunk + 1)], ps_av[0:64, :]), inc="se")
                        p.wait("vector", "sa", ca)
                        dslot = dnst[idx % 2]
                        if idx >= 2:
                            p.wait("vector", "sd", (idx - 1) * 16)
                        p.op("vector", lambda e, dslot=dslot: e.tensor_copy(dslot[:, :], ps_av[64:65, :]), inc="se")
                        cse = p.counts["se"]
                        p.wait("sync", "se", cse)
                        p.dma("sync", dnm[h:h + 1, :], dslot[:, :], inc="sd")
                        if p.grab_target == (sb, blk, h):
                            gs = p.sb("grabst", [128, 512], F32)
                            dbg2 = p.grab_dram
                            col = [0]
                            def dump(src_ap, ncols, rows=128):
                                for j0 in range(0, ncols, 512):
                                    w = min(512, ncols - j0)
                                    cg = p.op("vector", lambda e, src_ap=src_ap, j0=j0, w=w, rows=rows:
                                              e.tensor_copy(gs[0:rows, 0:w], src_ap[0:rows, j0:j0 + w]), inc="se")
                                    p.wait("sync", "se", cg)
                                    cgd = p.dma("sync", dbg2[0:rows, col[0]:col[0] + w], gs[0:rows, 0:w], inc="sd")
                                    p.wait("vector", "sd", cgd)
                                    col[0] += w
                            dump(ebuse, 4096)
                            dump(pT, 4096)
                            dump(kT[:, NW * mchunk:NW * (mchunk + 1)], 1536)
                            dump(qT[:, 1024 * mchunk:1024 * (mchunk + 1)], 1024)
                            dump(avS[:, 512 * mchunk:512 * (mchunk + 1)], 512)
                            dump(dnm[0:H, :], 512, rows=H)

                    cd = p.counts["sd"]
                    p.wait("vector", "sd", cd)
                    p.seq("vector", lambda e: e.tensor_scalar(dnm[:, :], dnm[:, :], skexp[0:H, 0:1], None, ALU.add))
                    p.seq("vector", lambda e: e.reciprocal(dnm[:, :], dnm[:, :]))
                    cv = p.op("vector", lambda e: e.tensor_copy(dnmb[:, :], dnm[:, :]), inc="ev")
                    p.wait("sync", "ev", cv)
                    dnd = bass.AP(g["dn_d"].tensor, 0, [[512, H], [1, 512]])
                    p.dma("sync", dnd, dnmb[:, :], inc="s2")
                    cd2 = p.counts["s2"]
                    resc = {}
                    for h in range(H):
                        mchunk, rbase = h // 2, 64 * (h % 2)
                        db = dnb[h % 2]
                        if h < 2:
                            p.wait("sync", "s2", cd2)
                        else:
                            p.wait("sync", "wo", resc[h - 2])
                        p.dma("sync", db[:, :],
                              bass.AP(g["dn_d"].tensor, h * 512, [[0, 128], [1, 512]]), inc="s2")
                        cdh = p.counts["s2"]
                        p.wait("vector", "s2", cdh)
                        resc[h] = p.op("vector", lambda e, mchunk=mchunk, rbase=rbase, db=db: e.tensor_mul(
                            avF[rbase:rbase + 64, 512 * mchunk:512 * (mchunk + 1)],
                            avS[rbase:rbase + 64, 512 * mchunk:512 * (mchunk + 1)],
                            db[rbase:rbase + 64, :]), inc="wo")
                    cww = p.counts["wo"]
                    ti = 0
                    acdone = {}
                    ev0 = p.counts["ev"]
                    for mq in range(8):
                        p.wait("sync", "sm", p.counts["sm"])
                        for kk in range(16):
                            p.dma("sync", wo4[:, 256 * kk:256 * (kk + 1)],
                                  g["Wo_bf"][kk * 128:(kk + 1) * 128, 256 * mq:256 * (mq + 1)], inc="lw")
                        cw = p.counts["lw"]
                        p.wait("tensor", "lw", cw)
                        if mq == 0:
                            p.wait("tensor", "wo", cww)
                        for m in range(2):
                            mo = mq * 2 + m
                            ps = ps_s[ti % 2]
                            if ti >= 2:
                                p.wait("tensor", "ev", ev0 + ti - 1)
                            for kk in range(16):
                                p.op("tensor", lambda e, kk=kk, m=m, ps=ps:
                                     e.matmul(ps[:, 0:512],
                                              wo4[:, 256 * kk + 128 * m:256 * kk + 128 * (m + 1)],
                                              avF[:, 512 * kk:512 * (kk + 1)],
                                              start=(kk == 0), stop=(kk == 15)),
                                     inc=("sm" if kk == 15 else None))
                            cm = p.counts["sm"]
                            sslot = stg[ti % 2]
                            p.wait("vector", "sm", cm)
                            if ti >= 2:
                                p.wait("vector", "ac", acdone[ti - 2])
                            p.op("vector", lambda e, ps=ps, sslot=sslot: e.tensor_copy(sslot[:, :], ps[:, 0:512]), inc="ev")
                            cse2 = p.counts["ev"]
                            p.wait("sync", "ev", cse2)
                            acdone[ti] = p.dma("sync",
                                               g["acc0"][128 * mo:128 * (mo + 1), WIN * gblk:WIN * (gblk + 1)],
                                               sslot[:, :], inc="ac")
                            ti += 1
                    p.wait("sync", "ac", p.counts["ac"])
                    p.barrier()
        p.barrier()
    _ascope.__exit__(None, None, None)
def emit_moe_l0(p, g, cst, idf):
    nc = p.nc
    g["combT_d"] = nc.dram_tensor("combT_d", [E, TL], F32, kind="Internal").ap()
    for snm in ("me", "ml", "mm2", "mev", "mac"):
        p.sem(snm)
    _mscope = p.scope()
    _mscope.__enter__()
    xnf = p.sb("xnf", [128, 16 * 2048], BF16)
    for kk in range(16):
        p.dma("sync", xnf[:, 2048 * kk:2048 * (kk + 1)],
              g["xnT_d"][kk * 128:(kk + 1) * 128, :], inc="ml")
    c_xnf = p.counts["ml"]

    with p.scope():
        gateb = p.sb("gateb", [128, 16 * 8], BF16)
        gatlo = p.sb("gatlo", [128, 16 * 8], BF16)
        gatf = p.sb("gatf", [128, 16 * 8], F32)
        xlo = p.sb("xlot", [128, 16 * 128], BF16)
        ps_g = p.ps("ps_g", [128, 8], F32)
        ps_t = p.ps("ps_t", [8, 128], F32)
        sg = p.sb("sgate", [128, 8], F32)
        ssum = p.sb("ssum", [128, 1], F32)
        m1 = p.sb("m1c", [128, 1], F32)
        t1 = p.sb("t1g", [128, 8], F32)
        combT = p.sb("combT", [8, 16 * 128], F32)
        for kk in range(16):
            p.dma("sync", gatf[:, 8 * kk:8 * (kk + 1)],
                  g["gateT"][kk * 128:(kk + 1) * 128, :], inc="ml")
        cgl = p.counts["ml"]
        p.wait("vector", "ml", cgl)
        p.seq("vector", lambda e: e.tensor_copy(gateb[:, :], gatf[:, :]))
        p.op("vector", lambda e: e.tensor_sub(gatlo[:, :], gatf[:, :], gateb[:, :]), inc="me")
        c_gl = p.counts["me"]
        p.wait("tensor", "me", c_gl)
        for tt in range(16):
            if tt > 0:
                p.wait("tensor", "mev", p.counts["mev"])
            cxl = p.dma("sync", xlo[:, :].rearrange("q (k c) -> q k c", c=128),
                        g["xnTlo_d"][:, 128 * tt:128 * (tt + 1)].rearrange("(k q) c -> q k c", q=128),
                        inc="ml")
            p.wait("tensor", "ml", cxl)
            for kk in range(16):
                p.op("tensor", lambda e, kk=kk, tt=tt: e.matmul(
                    ps_g[:, :], xnf[:, 2048 * kk + 128 * tt:2048 * kk + 128 * (tt + 1)],
                    gateb[:, 8 * kk:8 * (kk + 1)],
                    start=(kk == 0), stop=False), inc=None)
            for kk in range(16):
                p.op("tensor", lambda e, kk=kk, tt=tt: e.matmul(
                    ps_g[:, :], xnf[:, 2048 * kk + 128 * tt:2048 * kk + 128 * (tt + 1)],
                    gatlo[:, 8 * kk:8 * (kk + 1)],
                    start=False, stop=False), inc=None)
            for kk in range(16):
                p.op("tensor", lambda e, kk=kk, tt=tt: e.matmul(
                    ps_g[:, :], xlo[:, 128 * kk:128 * (kk + 1)],
                    gateb[:, 8 * kk:8 * (kk + 1)],
                    start=False, stop=(kk == 15)), inc=("mm2" if kk == 15 else None))
            cm = p.counts["mm2"]
            p.wait("scalar", "mm2", cm)
            p.op("scalar", lambda e: e.activation(sg[:, :], ps_g[:, :], AF.Exp,
                                                  accum_out=ssum[:, 0:1]), inc="me")
            cme = p.counts["me"]
            p.wait("vector", "me", cme)
            p.seq("vector", lambda e: e.reciprocal(ssum[:, :], ssum[:, :]))
            p.seq("vector", lambda e: e.tensor_scalar(sg[:, :], sg[:, :], ssum[:, 0:1], None, ALU.mult))
            p.seq("vector", lambda e: e.reduce_max(m1[:, 0:1], sg[:, :], axis=AX.X))
            p.seq("vector", lambda e: e.tensor_scalar(t1[:, :], sg[:, :], m1[:, 0:1], None, ALU.is_equal))
            p.seq("vector", lambda e: e.scalar_tensor_tensor(t1[:, :], t1[:, :], -1e30, sg[:, :], ALU.mult, ALU.add))
            p.seq("vector", lambda e: e.reduce_max(m1[:, 0:1], t1[:, :], axis=AX.X))
            p.seq("vector", lambda e: e.tensor_scalar(t1[:, :], sg[:, :], m1[:, 0:1], None, ALU.is_ge))
            cme = p.op("vector", lambda e: e.tensor_mul(sg[:, :], sg[:, :], t1[:, :]), inc="me")
            p.wait("tensor", "me", cme)
            p.op("tensor", lambda e: e.transpose(ps_t[:, :], sg[:, :], idf[:, :]), inc="mm2")
            cm2 = p.counts["mm2"]
            p.wait("vector", "mm2", cm2)
            p.op("vector", lambda e, tt=tt: e.tensor_copy(combT[:, 128 * tt:128 * (tt + 1)], ps_t[:, :]), inc="mev")
        cme = p.counts["mev"]
        p.wait("sync", "mev", cme)
        p.dma("sync", g["combT_d"], combT[:, :], inc="ml")
        p.wait("sync", "ml", p.counts["ml"])
    p.barrier()

    with p.scope():
        w1r = p.sb("w1r", [128, 16 * 1024], BF16)
        w3r = p.sb("w3r", [128, 16 * 1024], BF16)
        w2h = p.sb("w2h", [128, 8 * 1024], BF16)
        gT = p.sb("gT", [128, 8 * 2048], BF16)
        cbc = p.sb("cbc", [128, 2048], BF16)
        hsil = p.sb("hsil", [128, 512], BF16)
        pp = [p.ps(f"ps_m{j}", [128, 512], F32) for j in range(6)]
        stg = [p.sb(f"mstg{j}", [128, 2048], F32) for j in range(2)]
        acc_last = {}
        for e_ in range(E + 1):
            shared_e = (e_ == E)
            cl0 = p.counts["ml"]
            for kk in range(16):
                src1 = g["ws1T_bf"] if shared_e else g["w1T_bf"][e_]
                src3 = g["ws3T_bf"] if shared_e else g["w3T_bf"][e_]
                p.dma("sync", w1r[:, 1024 * kk:1024 * (kk + 1)],
                      src1[kk * 128:(kk + 1) * 128, :], inc="ml")
                p.dma("sync", w3r[:, 1024 * kk:1024 * (kk + 1)],
                      src3[kk * 128:(kk + 1) * 128, :], inc="ml")
            if not shared_e:
                p.dma("gpsimd", cbc[:, :],
                      bass.AP(g["combT_d"].tensor, e_ * TL, [[0, 128], [1, TL]]), inc="ml")
            c_w13 = p.counts["ml"]
            p.wait("tensor", "ml", c_w13)
            ti = 0
            mev0 = p.counts["mev"]
            for m in range(8):
                for nj in range(4):
                    ps1 = pp[(2 * ti) % 6]
                    ps3 = pp[(2 * ti + 1) % 6]
                    if ti >= 3:
                        p.wait("tensor", "mev", mev0 + 2 * (ti - 3) + 2)
                    for kk in range(16):
                        p.op("tensor", lambda e, kk=kk, m=m, nj=nj, ps1=ps1: e.matmul(
                            ps1[:, :], w1r[:, 1024 * kk + 128 * m:1024 * kk + 128 * (m + 1)],
                            xnf[:, 2048 * kk + 512 * nj:2048 * kk + 512 * (nj + 1)],
                            start=(kk == 0), stop=(kk == 15)), inc=("mm2" if kk == 15 else None))
                    c1 = p.counts["mm2"]
                    for kk in range(16):
                        p.op("tensor", lambda e, kk=kk, m=m, nj=nj, ps3=ps3: e.matmul(
                            ps3[:, :], w3r[:, 1024 * kk + 128 * m:1024 * kk + 128 * (m + 1)],
                            xnf[:, 2048 * kk + 512 * nj:2048 * kk + 512 * (nj + 1)],
                            start=(kk == 0), stop=(kk == 15)), inc=("mm2" if kk == 15 else None))
                    c3 = p.counts["mm2"]
                    p.wait("scalar", "mm2", c1)
                    if ti >= 1:
                        p.wait("scalar", "mev", mev0 + 2 * (ti - 1) + 1)
                    p.op("scalar", lambda e, ps1=ps1: e.activation(hsil[:, :], ps1[:, :], AF.Silu), inc="me")
                    cme = p.counts["me"]
                    p.wait("vector", "me", cme)
                    p.wait("vector", "mm2", c3)
                    gsl = gT[:, 2048 * m + 512 * nj:2048 * m + 512 * (nj + 1)]
                    p.op("vector", lambda e, ps3=ps3, gsl=gsl: e.tensor_mul(gsl, hsil[:, :], ps3[:, :]), inc="mev")
                    if not shared_e:
                        p.op("vector", lambda e, gsl=gsl, nj=nj: e.tensor_mul(
                            gsl, gsl, cbc[:, 512 * nj:512 * (nj + 1)]), inc="mev")
                    else:
                        p.op("vector", lambda e: e.engine_nop(), inc="mev")
                    ti += 1
            c_gdone_mm = p.counts["mm2"]
            c_gdone_ev = p.counts["mev"]
            for mdh in range(2):
                src2 = g["ws2T_bf"] if shared_e else g["w2T_bf"][e_]
                p.wait("sync", "mm2", p.counts["mm2"])
                for mk in range(8):
                    p.dma("sync", w2h[:, 1024 * mk:1024 * (mk + 1)],
                          src2[mk * 128:(mk + 1) * 128, 1024 * mdh:1024 * (mdh + 1)], inc="ml")
                cw2 = p.counts["ml"]
                p.wait("tensor", "ml", cw2)
                p.wait("tensor", "mev", c_gdone_ev)
                eo_ev = {}
                for md in range(8):
                    mdg = mdh * 8 + md
                    sslot = stg[mdg % 2]
                    for nj in range(4):
                        ps = pp[nj % 2]
                        slot_prev = eo_ev.get(nj % 2)
                        if slot_prev is not None:
                            p.wait("tensor", "mev", slot_prev)
                        for mk in range(8):
                            p.op("tensor", lambda e, mk=mk, md=md, nj=nj, ps=ps: e.matmul(
                                ps[:, :], w2h[:, 1024 * mk + 128 * md:1024 * mk + 128 * (md + 1)],
                                gT[:, 2048 * mk + 512 * nj:2048 * (mk) + 512 * (nj + 1)],
                                start=(mk == 0), stop=(mk == 7)), inc=("mm2" if mk == 7 else None))
                        cm = p.counts["mm2"]
                        p.wait("vector", "mm2", cm)
                        if nj == 0:
                            sl = acc_last.get(("slot", mdg % 2))
                            if sl is not None:
                                p.wait("vector", "mac", sl)
                        eo_ev[nj % 2] = p.op("vector", lambda e, ps=ps, nj=nj, sslot=sslot: e.tensor_copy(
                            sslot[:, 512 * nj:512 * (nj + 1)], ps[:, :]), inc="mev")
                    cme = p.counts["mev"]
                    p.wait("gpsimd", "mev", cme)
                    la = acc_last.get(mdg)
                    if la is not None:
                        p.wait("gpsimd", "mac", la)
                    acc_last[mdg] = p.dma("gpsimd", g["acc0"][128 * mdg:128 * (mdg + 1), :],
                                          sslot[:, :], inc="mac", accum_op=ALU.add)
                    acc_last[("slot", mdg % 2)] = acc_last[mdg]
            p.barrier()
    p.wait("gpsimd", "mac", p.counts["mac"])
    _mscope.__exit__(None, None, None)


def emit_l1_prelude(p, g, cst, idf, idb, u):
    for snm in ("pl", "pv", "ps_", "pt", "pe"):
        p.sem(snm)
    with p.scope():
        xt = p.sb("pxt", [128, DM], F32)
        ac0 = p.sb("pac0", [128, DM], F32)
        x1t = p.sb("px1t", [128, DM], F32)
        xn1t = p.sb("pxn1t", [128, DM], F32)
        xnb = p.sb("pxnb", [128, DM], BF16)
        ub = p.sb("pub", [128, DM], BF16)
        wbc = p.sb("pwbc", [128, DM], F32)
        bbc = p.sb("pbbc", [128, DM], F32)
        stats = p.sb("psta", [128, 24], F32)
        aggr = p.sb("pagg", [128, 2], F32)
        rs = p.sb("prs", [128, 1], F32)
        mm = p.sb("pmm", [128, 1], F32)
        stgf = p.sb("pstgf", [128, DM], F32)
        stgb = p.sb("pstgb", [128, DM], BF16)
        psf = [p.ps(f"ppsf{j}", [128, 128], F32) for j in range(2)]
        psb = [p.ps(f"ppsb{j}", [128, 128], BF16) for j in range(2)]
        p.dma("sync", wbc[:, :], bass.AP(g["ln1w"].tensor, 0, [[0, 128], [1, DM]]), inc="pl")
        p.dma("sync", bbc[:, :], bass.AP(g["ln1b"].tensor, 0, [[0, 128], [1, DM]]), inc="pl")
        ev_f = {}
        ev_b = {}
        for i in range(16):
            if i > 0:
                p.wait("sync", "pl", p.counts["pl"])
            p.dma("sync", xt[:, :], g["x"][128 * i:128 * (i + 1), :], inc="pl")
            cl = p.dma("sync", ac0[:, :].rearrange("q (k c) -> q k c", c=128),
                       g["acc0"][:, 128 * i:128 * (i + 1)].rearrange("(k q) c -> q k c", q=128),
                       inc="pl")
            p.wait("tensor", "pl", cl)
            for k in range(16):
                ps = psf[k % 2]
                if ev_f.get(k % 2) is not None:
                    p.wait("tensor", "pe", ev_f[k % 2])
                p.op("tensor", lambda e, k=k, ps=ps: e.transpose(
                    ps[:, :], ac0[:, 128 * k:128 * (k + 1)], idf[:, :]), inc="pt")
                ct = p.counts["pt"]
                p.wait("vector", "pt", ct)
                ev_f[k % 2] = p.op("vector", lambda e, k=k, ps=ps: e.tensor_add(
                    x1t[:, 128 * k:128 * (k + 1)], xt[:, 128 * k:128 * (k + 1)], ps[:, :]), inc="pe")
            p.wait("vector", "pe", p.counts["pe"])
            for j in range(4):
                p.seq("vector", lambda e, j=j: e.bn_stats(
                    stats[:, 6 * j:6 * j + 6], x1t[:, 512 * j:512 * (j + 1)]))
            cs = p.op("vector", lambda e: e.bn_aggr(aggr[:, :], stats[:, :]), inc="pv")
            p.wait("scalar", "pv", cs)
            p.seq("scalar", lambda e: e.activation(rs[:, :], aggr[:, 1:2], AF.Ln, bias=cst[0:128, 1:2]))
            p.seq("scalar", lambda e: e.activation(rs[:, :], rs[:, :], AF.Exp, scale=-0.5))
            p.seq("scalar", lambda e: e.activation(mm[:, :], aggr[:, 0:1], AF.Identity, scale=rs[:, 0:1]))
            p.seq("scalar", lambda e: e.activation(mm[:, :], mm[:, :], AF.Identity, scale=-1.0))
            p.wait("sync", "pe", p.counts["pe"])
            cx1 = p.dma("sync", g["x1o"][128 * i:128 * (i + 1), :], x1t[:, :], inc="pl")
            cs = p.op("scalar", lambda e: e.activation(
                xn1t[:, :], x1t[:, :], AF.Identity, bias=mm[:, 0:1], scale=rs[:, 0:1]), inc="ps_")
            p.wait("vector", "ps_", cs)
            p.wait("vector", "pl", cx1)
            p.seq("vector", lambda e: e.tensor_copy(xnb[:, :], xn1t[:, :]))
            p.seq("vector", lambda e: e.tensor_mul(x1t[:, :], xn1t[:, :], wbc[:, :]))
            cv = p.op("vector", lambda e: e.tensor_add(ub[:, :], x1t[:, :], bbc[:, :]), inc="pv")
            p.wait("tensor", "pv", cv)
            for k in range(16):
                ps = psf[k % 2]
                pb = psb[k % 2]
                if ev_f.get(k % 2) is not None:
                    p.wait("tensor", "pe", ev_f[k % 2])
                p.op("tensor", lambda e, k=k, ps=ps: e.transpose(
                    ps[:, :], xn1t[:, 128 * k:128 * (k + 1)], idf[:, :]), inc="pt")
                ct = p.counts["pt"]
                p.wait("vector", "pt", ct)
                ev_f[k % 2] = p.op("vector", lambda e, k=k, ps=ps: e.tensor_copy(
                    stgf[:, 128 * k:128 * (k + 1)], ps[:, :]), inc="pe")
                if ev_b.get(k % 2) is not None:
                    p.wait("tensor", "pe", ev_b[k % 2])
                p.op("tensor", lambda e, k=k, pb=pb: e.transpose(
                    pb[:, :], xnb[:, 128 * k:128 * (k + 1)], idb[:, :]), inc="pt")
                ct = p.counts["pt"]
                p.wait("scalar", "pt", ct)
                c1 = p.op("scalar", lambda e, k=k, pb=pb: e.copy(
                    stgb[:, 128 * k:128 * (k + 1)], pb[:, :]), inc="pe")
                p.wait("tensor", "pe", c1)
                p.op("tensor", lambda e, k=k, pb=pb: e.transpose(
                    pb[:, :], ub[:, 128 * k:128 * (k + 1)], idb[:, :]), inc="pt")
                ct = p.counts["pt"]
                p.wait("scalar", "pt", ct)
                ev_b[k % 2] = p.op("scalar", lambda e, k=k, i=i, pb=pb: e.copy(
                    u[:, 2048 * k + 128 * i:2048 * k + 128 * (i + 1)], pb[:, :]), inc="pe")
            ce = p.counts["pe"]
            p.wait("sync", "pe", ce)
            p.dma("sync", g["acc1"][:, 128 * i:128 * (i + 1)].rearrange("(k q) c -> q k c", q=128),
                  stgf[:, :].rearrange("q (k c) -> q k c", c=128), inc="pl")
            p.dma("sync", g["xn1T_d"][:, 128 * i:128 * (i + 1)].rearrange("(k q) c -> q k c", q=128),
                  stgb[:, :].rearrange("q (k c) -> q k c", c=128), inc="pl")
        p.wait("sync", "pl", p.counts["pl"])
    p.barrier()


def emit_inproj(p, g, u):
    for snm in ("il", "im", "ie", "ia", "cc"):
        p.sem(snm)
    with p.scope():
        wq = p.sb("iwq", [128, 16 * 512], BF16)
        stg = [p.sb(f"istg{j}", [128, 512], BF16) for j in range(2)]
        pp = [p.ps(f"ips{j}", [128, 512], F32) for j in range(4)]
        ti = 0
        acd = {}
        ev = {}
        for mq in range(16):
            p.wait("sync", "im", p.counts["im"])
            for kk in range(16):
                p.dma("sync", wq[:, 512 * kk:512 * (kk + 1)],
                      g["inpT_bf"][kk * 128:(kk + 1) * 128, 512 * mq:512 * (mq + 1)], inc="il")
            cw = p.counts["il"]
            p.wait("tensor", "il", cw)
            for m in range(4):
                mo = mq * 4 + m
                for nj in range(4):
                    ps = pp[ti % 4]
                    if ev.get(ti % 4) is not None:
                        p.wait("tensor", "ie", ev[ti % 4])
                    for kk in range(16):
                        p.op("tensor", lambda e, kk=kk, m=m, nj=nj, ps=ps: e.matmul(
                            ps[:, :], wq[:, 512 * kk + 128 * m:512 * kk + 128 * (m + 1)],
                            u[:, 2048 * kk + 512 * nj:2048 * kk + 512 * (nj + 1)],
                            start=(kk == 0), stop=(kk == 15)), inc=("im" if kk == 15 else None))
                    cm = p.counts["im"]
                    sl = stg[ti % 2]
                    p.wait("scalar", "im", cm)
                    if ti >= 2:
                        p.wait("scalar", "ia", acd[ti - 2])
                    if mo < 32:
                        ev[ti % 4] = p.op("scalar", lambda e, ps=ps, sl=sl: e.copy(sl[:, :], ps[:, :]), inc="ie")
                    else:
                        ev[ti % 4] = p.op("scalar", lambda e, ps=ps, sl=sl: e.activation(sl[:, :], ps[:, :], AF.Silu), inc="ie")
                    ce = p.counts["ie"]
                    p.wait("sync", "ie", ce)
                    if mo < 32:
                        acd[ti] = p.dma("sync", g["ag1in"][128 * mo:128 * (mo + 1), 512 * nj:512 * (nj + 1)],
                                        sl[:, :], inc="ia")
                    else:
                        acd[ti] = p.dma("sync", g["zsil_d"][128 * (mo - 32):128 * (mo - 31), 512 * nj:512 * (nj + 1)],
                                        sl[:, :], inc="ia")
                    ti += 1
        p.wait("gpsimd", "ia", p.counts["ia"])
    p.barrier()


def emit_scan_core(p, g, cst):
    TT = 16384
    _sscope = p.scope()
    _sscope.__enter__()
    p.sem("sb2")
    idx_unused = None
    dtb = p.sb("dtbb", [128, 4], F32)
    aneg = p.sb("anegb", [128, 4 * DS], F32)
    dskip = p.sb("dskipb", [128, 4], F32)
    dtw = p.sb("dtwb", [128, 4 * 128], BF16)
    carry = p.sb("carryb", [128, DS], F32)
    p.dma("sync", dtb[:, :].rearrange("q (d c) -> q d c", c=1),
          g["dtb_s"].rearrange("(d q) c -> q d c", q=128), inc="sl",
          allow_slow_non_contiguous=True)
    p.dma("sync", aneg[:, :].rearrange("q (d c) -> q d c", c=16),
          g["alog_s"].rearrange("(d q) c -> q d c", q=128), inc="sl")
    p.dma("sync", dskip[:, :].rearrange("q (d c) -> q d c", c=1),
          g["dskip_s"].rearrange("(d q) c -> q d c", q=128), inc="sl",
          allow_slow_non_contiguous=True)
    p.dma("gpsimd", dtw[:, :], g["dtwT_s"], inc="sl")
    c_const = p.counts["sl"]
    p.wait("scalar", "sl", c_const)
    p.seq("scalar", lambda e: e.activation(aneg[:, :], aneg[:, :], AF.Exp))
    p.op("scalar", lambda e: e.activation(aneg[:, :], aneg[:, :], AF.Identity, scale=-1.0), inc="sc2")
    c_aneg = p.counts["sc2"]

    with p.scope():
        xdr = p.sb("xdrb", [128, TT], BF16)
        p.dma("gpsimd", xdr[:, :], g["xdbl_d"][0:128, :], inc="sl")
        c_xdr = p.counts["sl"]
        dtt = p.sb("dttb", [128, TT], BF16)
        sp1 = p.sb("sp1b", [128, 1024], F32)
        sp2 = p.sb("sp2b", [128, 1024], F32)
        ps_dt = [p.ps(f"ps_dt{j}", [128, 512], F32) for j in range(2)]
        ab = p.sb("abb", [128, 4096], F32)
        uxc = p.sb("uxcb", [128, 4096], BF16)
        bb = [p.sb(f"bbb{j}", [128, 4096], BF16) for j in range(2)]
        hb = [p.sb(f"hbb{j}", [128, 4096], BF16) for j in range(2)]
        pb = p.sb("pbb", [128, 4096], BF16)
        yacc = p.sb("yaccb", [128, 4096], F32)
        bbc = [p.sb(f"bbcb{j}", [128, 4096], BF16) for j in range(2)]
        cbc = [p.sb(f"cbcb{j}", [128, 4096], BF16) for j in range(2)]
        xcs_one = p.sb("xcsb", [128, 4096], BF16)
        xcs = [xcs_one, xcs_one]
        ypb = p.sb("ypbb", [128, 4096], BF16)
        p.wait("tensor", "sl", c_xdr)
        for dt_i in range(4):
            ev = {}
            c_sv0 = p.counts["sv2"]
            for ncj in range(32):
                ps = ps_dt[ncj % 2]
                if ev.get(ncj % 2) is not None:
                    p.wait("tensor", "sc2", ev[ncj % 2])
                if ncj >= 2:
                    p.wait("scalar", "sv2", c_sv0 + ncj - 1)
                p.op("tensor", lambda e, ncj=ncj, dt_i=dt_i, ps=ps: e.matmul(
                    ps[:, :], dtw[:, 128 * dt_i:128 * (dt_i + 1)],
                    xdr[:, 512 * ncj:512 * (ncj + 1)], start=True, stop=True), inc="sm2")
                cm = p.counts["sm2"]
                p.wait("scalar", "sm2", cm)
                p.seq("scalar", lambda e, ps=ps, dt_i=dt_i, ncj=ncj: e.activation(
                    sp1[:, 512 * (ncj % 2):512 * (ncj % 2 + 1)], ps[:, :], AF.Abs, bias=dtb[:, dt_i:dt_i + 1]))
                ev[ncj % 2] = p.op("scalar", lambda e, ps=ps, dt_i=dt_i, ncj=ncj: e.activation(
                    sp2[:, 512 * (ncj % 2):512 * (ncj % 2 + 1)], ps[:, :], AF.Relu, bias=dtb[:, dt_i:dt_i + 1]), inc="sc2")
                p.seq("scalar", lambda e, ncj=ncj: e.activation(
                    sp1[:, 512 * (ncj % 2):512 * (ncj % 2 + 1)], sp1[:, 512 * (ncj % 2):512 * (ncj % 2 + 1)],
                    AF.Exp, scale=-1.0))
                cs = p.op("scalar", lambda e, ncj=ncj: e.activation(
                    sp1[:, 512 * (ncj % 2):512 * (ncj % 2 + 1)], sp1[:, 512 * (ncj % 2):512 * (ncj % 2 + 1)],
                    AF.Ln, bias=1.0), inc="sc2")
                p.wait("vector", "sc2", cs)
                p.op("vector", lambda e, ncj=ncj: e.tensor_add(
                    dtt[:, 512 * ncj:512 * (ncj + 1)],
                    sp1[:, 512 * (ncj % 2):512 * (ncj % 2 + 1)],
                    sp2[:, 512 * (ncj % 2):512 * (ncj % 2 + 1)]), inc="sv2")
            c_dt = p.counts["sv2"]
            p.wait("vector", "sv2", c_dt)
            p.wait("scalar", "sv2", c_dt)
            p.seq("vector", lambda e: e.memset(carry[:, :], 0.0))
            last_cyp = None
            for tc in range(4):
                if tc == 2:
                    p.seq("vector", lambda e: e.memset(carry[:, :], 0.0))
                if last_cyp is not None:
                    p.wait("sync", "sv2", last_cyp)
                cxl = p.dma("sync", xcs[tc % 2][:, :], g["xc_d"][dt_i][:, 4096 * tc:4096 * (tc + 1)], inc="sl")
                p.wait("vector", "sl", cxl)
                p.seq("vector", lambda e, tc=tc: e.tensor_mul(
                    uxc[:, :], dtt[:, 4096 * tc:4096 * (tc + 1)], xcs[tc % 2][:, :]))
                p.seq("vector", lambda e: e.memset(yacc[:, :], 0.0))
                last_cvp = None
                cvp_hist = {}
                for si in range(DS):
                    bsl = bb[si % 2]
                    hsl = hb[si % 2]
                    if si >= 2:
                        p.wait("gpsimd", "sv2", cvp_hist[si % 2])
                        p.wait("sync", "sv2", cvp_hist[si % 2])
                    cbl = p.dma("gpsimd", bbc[si % 2][:, :],
                                bass.AP(g["xdblb_d"].tensor, si * TT + 4096 * tc,
                                        [[0, 128], [1, 4096]]), inc="sg2")
                    cb2 = p.dma("sync", cbc[si % 2][:, :],
                                bass.AP(g["xdblb_d"].tensor, (16 + si) * TT + 4096 * tc,
                                        [[0, 128], [1, 4096]]), inc="sb2")
                    p.wait("scalar", "sc2", c_aneg)
                    if last_cvp is not None:
                        p.wait("scalar", "sv2", last_cvp)
                    csa = p.op("scalar", lambda e, tc=tc, dt_i=dt_i, si=si: e.activation(
                        ab[:, :], dtt[:, 4096 * tc:4096 * (tc + 1)], AF.Exp,
                        scale=aneg[:, DS * dt_i + si:DS * dt_i + si + 1]), inc="sc2")
                    p.wait("vector", "sg2", cbl)
                    p.wait("vector", "sb2", cb2)
                    p.wait("vector", "sc2", csa)
                    p.op("vector", lambda e, bsl=bsl, si=si: e.tensor_mul(
                        bsl[:, :], uxc[:, :], bbc[si % 2][:, :]))
                    p.seq("vector", lambda e, bsl=bsl, hsl=hsl, si=si: e.tensor_tensor_scan(
                        hsl[:, :], ab[:, :], bsl[:, :], carry[:, si:si + 1], ALU.mult, ALU.add))
                    p.seq("vector", lambda e, hsl=hsl, si=si: e.tensor_copy(
                        carry[:, si:si + 1], hsl[:, 4095:4096]))
                    p.op("vector", lambda e, hsl=hsl, si=si: e.tensor_mul(
                        pb[:, :], hsl[:, :], cbc[si % 2][:, :]))
                    cvp = p.op("vector", lambda e: e.tensor_add(
                        yacc[:, :], yacc[:, :], pb[:, :]), inc="sv2")
                    last_cvp = cvp
                    cvp_hist[si % 2] = cvp
                p.seq("vector", lambda e, tc=tc, dt_i=dt_i: e.scalar_tensor_tensor(
                    ypb[:, :], xcs[tc % 2][:, :], dskip[:, dt_i:dt_i + 1], yacc[:, :],
                    ALU.mult, ALU.add))
                cyp = p.op("vector", lambda e: e.engine_nop(), inc="sv2")
                last_cyp = cyp
                p.wait("sync", "sv2", cyp)
                cyd = p.dma("sync", g["yo"][128 * dt_i:128 * (dt_i + 1), 4096 * tc:4096 * (tc + 1)],
                            ypb[:, :], inc="sl")
                p.wait("vector", "sl", cyd)
    _sscope.__exit__(None, None, None)
    p.barrier()


def emit_proj_big(p, g, wbf_d, rhs, n_out_tiles, writer, pref):
    li, mi, ei = pref + "l", pref + "m", pref + "e"
    pa = [pref + f"a{j}" for j in range(2)]
    for snm in (li, mi, ei, *pa):
        p.sem(snm)
    with p.scope():
        wq = p.sb(pref + "wq", [128, 16 * 512], BF16)
        stg = [p.sb(pref + f"st{j}", [128, 512], BF16) for j in range(2)]
        pp = [p.ps(pref + f"ps{j}", [128, 512], F32) for j in range(4)]
        ti = 0
        acd = {}
        ev = {}
        nq = cdiv(n_out_tiles, 4)
        for mq in range(nq):
            p.wait("sync", mi, p.counts[mi])
            mw = min(512, n_out_tiles * 128 - 512 * mq)
            for kk in range(16):
                p.dma("sync", wq[:, 512 * kk:512 * kk + mw],
                      wbf_d[kk * 128:(kk + 1) * 128, 512 * mq:512 * mq + mw], inc=li)
            cw = p.counts[li]
            p.wait("tensor", li, cw)
            for m in range(min(4, n_out_tiles - 4 * mq)):
                mo = mq * 4 + m
                for nj in range(4):
                    ps = pp[ti % 4]
                    if ev.get(ti % 4) is not None:
                        p.wait("tensor", ei, ev[ti % 4])
                    for kk in range(16):
                        p.op("tensor", lambda e, kk=kk, m=m, nj=nj, ps=ps: e.matmul(
                            ps[:, :], wq[:, 512 * kk + 128 * m:512 * kk + 128 * (m + 1)],
                            rhs[:, 2048 * kk + 512 * nj:2048 * kk + 512 * (nj + 1)],
                            start=(kk == 0), stop=(kk == 15)), inc=(mi if kk == 15 else None))
                    cm = p.counts[mi]
                    sl = stg[ti % 2]
                    p.wait("scalar", mi, cm)
                    if ti >= 2:
                        p.wait("scalar", pa[ti % 2], acd[ti - 2])
                    ev[ti % 4], acd[ti] = writer(mo, nj, ps, sl, ei, pa[ti % 2])
                    ti += 1
        p.wait("gpsimd", pa[0], p.counts[pa[0]])
        p.wait("gpsimd", pa[1], p.counts[pa[1]])
        p.wait("sync", pa[0], p.counts[pa[0]])
        p.wait("sync", pa[1], p.counts[pa[1]])


def emit_contract_acc(p, g, wT_dram, rhs_d, n_k, acc_d, acc_last, pref, rhs_rows=None):
    assert n_k % 2 == 0
    KG = n_k // 2
    if "ctr_scr" not in g:
        g["ctr_scr"] = p.nc.dram_tensor("ctr_scr", [2, DM, TL], F32, kind="Internal").ap()
    scr = g["ctr_scr"]
    mi, ei, lrs = pref + "m", pref + "e", pref + "lr"
    lw = [pref + f"lw{j}" for j in range(2)]
    po = [pref + f"po{j}" for j in range(2)]
    for snm in (mi, ei, lrs, *lw, *po):
        p.sem(snm)
    with p.scope():
        krhs = p.sb(pref + "kr", [128, KG * 2048], BF16)
        wq = [p.sb(pref + f"wq{j}", [128, KG * 128], BF16) for j in range(2)]
        pp = [p.ps(pref + f"ps{j}", [128, 512], F32) for j in range(4)]
        stg = [p.sb(pref + f"st{j}", [128, 2048], F32) for j in range(2)]
        for gi in range(2):
            if gi > 0:
                p.wait("sync", mi, p.counts[mi])
            for j in range(KG):
                kt = gi * KG + j
                p.dma("sync", krhs[:, 2048 * j:2048 * (j + 1)],
                      rhs_d[128 * kt:128 * (kt + 1), :], inc=lrs)
            crh = p.counts[lrs]
            wl = {}
            ev = {}
            pod = {}
            for m in range(16):
                wslot = wq[m % 2]
                if wl.get(m % 2) is not None:
                    p.wait("sync", mi, wl[m % 2])
                for j in range(KG):
                    kt = gi * KG + j
                    p.dma("sync", wslot[:, 128 * j:128 * (j + 1)],
                          wT_dram[128 * kt:128 * (kt + 1), 128 * m:128 * (m + 1)],
                          inc=lw[m % 2])
                cw = p.counts[lw[m % 2]]
                p.wait("tensor", lw[m % 2], cw)
                if m == 0:
                    p.wait("tensor", lrs, crh)
                for j in range(KG):
                    for nj in range(4):
                        if j == 0 and ev.get(nj) is not None:
                            p.wait("tensor", ei, ev[nj])
                        wl[m % 2] = p.op(
                            "tensor", lambda e, j=j, nj=nj, wslot=wslot, KG=KG:
                            e.matmul(pp[nj][:, :], wslot[:, 128 * j:128 * (j + 1)],
                                     krhs[:, 2048 * j + 512 * nj:2048 * j + 512 * (nj + 1)],
                                     start=(j == 0), stop=(j == KG - 1)),
                            inc=mi)
                cm = p.counts[mi]
                sslot = stg[m % 2]
                p.wait("vector", mi, cm)
                if pod.get(m % 2) is not None:
                    p.wait("vector", po[m % 2], pod[m % 2])
                for nj in range(4):
                    ev[nj] = p.op("vector", lambda e, nj=nj, sslot=sslot: e.tensor_copy(
                        sslot[:, 512 * nj:512 * (nj + 1)], pp[nj][:, :]), inc=ei)
                ce = p.counts[ei]
                p.wait("sync", ei, ce)
                pod[m % 2] = p.dma("sync", scr[gi][128 * m:128 * (m + 1), :],
                                   sslot[:, :], inc=po[m % 2])
            p.wait("sync", po[0], p.counts[po[0]])
            p.wait("sync", po[1], p.counts[po[1]])
    fa, fv = pref + "fa", pref + "fv"
    fo_ = [pref + f"fo{j}" for j in range(2)]
    for snm in (fa, fv, *fo_):
        p.sem(snm)
    with p.scope():
        s0 = [p.sb(pref + f"f0{j}", [128, 2048], F32) for j in range(2)]
        s1 = [p.sb(pref + f"f1{j}", [128, 2048], F32) for j in range(2)]
        fod = {}
        for m in range(16):
            if fod.get(m % 2) is not None:
                p.wait("sync", fo_[m % 2], fod[m % 2])
            p.dma("sync", s0[m % 2][:, :], scr[0][128 * m:128 * (m + 1), :], inc=fa)
            cl = p.dma("sync", s1[m % 2][:, :], scr[1][128 * m:128 * (m + 1), :], inc=fa)
            p.wait("vector", fa, cl)
            cv = p.op("vector", lambda e, m=m: e.tensor_add(
                s0[m % 2][:, :], s0[m % 2][:, :], s1[m % 2][:, :]), inc=fv)
            p.wait("gpsimd", fv, cv)
            fod[m % 2] = p.dma("gpsimd", acc_d[128 * m:128 * (m + 1), :],
                               s0[m % 2][:, :], inc=fo_[m % 2], accum_op=ALU.add)
        p.wait("gpsimd", fo_[0], p.counts[fo_[0]])
        p.wait("gpsimd", fo_[1], p.counts[fo_[1]])


def emit_ln2_v2T(p, g, cst, idf, idb, v2T):
    for snm in ("ql", "qv", "qs", "qt", "qe"):
        p.sem(snm)
    with p.scope():
        ogt = p.sb("qogt", [128, DM], F32)
        ac1 = p.sb("qac1", [128, DM], F32)
        v2t = p.sb("qv2t", [128, DM], BF16)
        wbc = p.sb("qwbc", [128, DM], F32)
        bbc = p.sb("qbbc", [128, DM], F32)
        stats = p.sb("qsta", [128, 24], F32)
        aggr = p.sb("qagg", [128, 2], F32)
        rs = p.sb("qrs", [128, 1], F32)
        mm = p.sb("qmm", [128, 1], F32)
        psf = [p.ps(f"qpsf{j}", [128, 128], F32) for j in range(2)]
        psb = [p.ps(f"qpsb{j}", [128, 128], BF16) for j in range(2)]
        p.dma("sync", wbc[:, :], bass.AP(g["ln2w"].tensor, 0, [[0, 128], [1, DM]]), inc="ql")
        p.dma("sync", bbc[:, :], bass.AP(g["ln2b"].tensor, 0, [[0, 128], [1, DM]]), inc="ql")
        ev_f = {}
        ev_b = {}
        for i in range(16):
            cl = p.dma("sync", ac1[:, :].rearrange("q (k c) -> q k c", c=128),
                       g["acc1"][:, 128 * i:128 * (i + 1)].rearrange("(k q) c -> q k c", q=128),
                       inc="ql")
            p.wait("tensor", "ql", cl)
            for k in range(16):
                ps = psf[k % 2]
                if ev_f.get(k % 2) is not None:
                    p.wait("tensor", "qe", ev_f[k % 2])
                p.op("tensor", lambda e, k=k, ps=ps: e.transpose(
                    ps[:, :], ac1[:, 128 * k:128 * (k + 1)], idf[:, :]), inc="qt")
                ct = p.counts["qt"]
                p.wait("vector", "qt", ct)
                ev_f[k % 2] = p.op("vector", lambda e, k=k, ps=ps: e.tensor_copy(
                    ogt[:, 128 * k:128 * (k + 1)], ps[:, :]), inc="qe")
            p.wait("vector", "qe", p.counts["qe"])
            for j in range(4):
                p.seq("vector", lambda e, j=j: e.bn_stats(
                    stats[:, 6 * j:6 * j + 6], ogt[:, 512 * j:512 * (j + 1)]))
            cs = p.op("vector", lambda e: e.bn_aggr(aggr[:, :], stats[:, :]), inc="qv")
            p.wait("scalar", "qv", cs)
            p.seq("scalar", lambda e: e.activation(rs[:, :], aggr[:, 1:2], AF.Ln, bias=cst[0:128, 1:2]))
            p.seq("scalar", lambda e: e.activation(rs[:, :], rs[:, :], AF.Exp, scale=-0.5))
            p.seq("scalar", lambda e: e.activation(mm[:, :], aggr[:, 0:1], AF.Identity, scale=rs[:, 0:1]))
            p.seq("scalar", lambda e: e.activation(mm[:, :], mm[:, :], AF.Identity, scale=-1.0))
            cs = p.op("scalar", lambda e: e.activation(
                ac1[:, :], ogt[:, :], AF.Identity, bias=mm[:, 0:1], scale=rs[:, 0:1]), inc="qs")
            p.wait("vector", "qs", cs)
            p.seq("vector", lambda e: e.tensor_mul(ogt[:, :], ac1[:, :], wbc[:, :]))
            cv = p.op("vector", lambda e: e.tensor_add(v2t[:, :], ogt[:, :], bbc[:, :]), inc="qv")
            p.wait("tensor", "qv", cv)
            for k in range(16):
                pb = psb[k % 2]
                if ev_b.get(k % 2) is not None:
                    p.wait("tensor", "qe", ev_b[k % 2])
                p.op("tensor", lambda e, k=k, pb=pb: e.transpose(
                    pb[:, :], v2t[:, 128 * k:128 * (k + 1)], idb[:, :]), inc="qt")
                ct = p.counts["qt"]
                p.wait("scalar", "qt", ct)
                ev_b[k % 2] = p.op("scalar", lambda e, k=k, i=i, pb=pb: e.copy(
                    v2T[:, 2048 * k + 128 * i:2048 * k + 128 * (i + 1)], pb[:, :]), inc="qe")
            p.wait("sync", "qe", p.counts["qe"])
            p.wait("vector", "qe", p.counts["qe"])
    p.barrier()


def emit_quant_pass1(p, g):
    for snm in ("bl", "bm", "bv"):
        p.sem(snm)
    with p.scope():
        wt = p.sb("bwt", [128, BH], F32)
        rsum = p.sb("brsum", [128, 12], F32)
        racc = p.sb("bracc", [128, 4], F32)
        ones = p.sb("bones", [128, 1], F32)
        gst = p.sb("bgst", [1, 16], F32)
        ps_r = p.ps("bpsr", [1, 4], F32)
        p.seq("vector", lambda e: e.memset(ones[:, :], 1.0))
        p.seq("vector", lambda e: e.memset(racc[:, :], 0.0))
        p.seq("vector", lambda e: e.memset(gst[:, :], 0.0))
        specs = [("bw1T_s", DMPC, BH, 0), ("bw2T_s", DMPC, BH, 1), ("bw3T_s", BHPC, DM, 2)]
        for nm, rows, cols, ci in specs:
            nt = cdiv(rows, 128)
            for it in range(nt):
                r = min(128, rows - 128 * it)
                cl = p.dma("sync", wt[0:r, 0:cols], g[nm][128 * it:128 * it + r, :], inc="bl")
                p.wait("vector", "bl", cl)
                nch = cdiv(cols, 512)
                for j in range(nch):
                    w_ = min(512, cols - 512 * j)
                    p.seq("vector", lambda e, r=r, j=j, w_=w_: e.reduce_sum(
                        rsum[0:r, j:j + 1], wt[0:r, 512 * j:512 * j + w_],
                        axis=AX.X, apply_absolute_value=True))
                p.seq("vector", lambda e, r=r, nch=nch: e.reduce_sum(
                    rsum[0:r, 0:1], rsum[0:r, 0:nch], axis=AX.X))
                p.seq("vector", lambda e, r=r, ci=ci: e.tensor_add(
                    racc[0:r, ci:ci + 1], racc[0:r, ci:ci + 1], rsum[0:r, 0:1]))
        cv = p.op("vector", lambda e: e.engine_nop(), inc="bv")
        p.wait("tensor", "bv", cv)
        p.op("tensor", lambda e: e.matmul(ps_r[:, :], ones[:, :], racc[:, 0:4],
                                          start=True, stop=True), inc="bm")
        cm = p.counts["bm"]
        p.wait("vector", "bm", cm)
        cv = p.op("vector", lambda e: e.tensor_copy(gst[:, 0:4], ps_r[:, :]), inc="bv")
        p.wait("sync", "bv", cv)
        p.dma("sync", g["gamo"], gst[:, :], inc="bl")
        p.wait("sync", "bl", p.counts["bl"])
        p.wait("vector", "bl", p.counts["bl"])
    p.barrier()


def emit_quant_pass2(p, g, cst):
    for snm in ("bl", "bv", "bs", "be"):
        p.sem(snm)
    with p.scope():
        wt = p.sb("bwt", [128, BH], F32)
        wa = p.sb("bwa", [128, BH], F32)
        sg = p.sb("bsg", [128, BH], F32)
        qo = p.sb("bqo", [128, BH], BF16)
        gall = p.sb("bgall", [1, 128], F32)
        gsum = p.sb("bgsum", [1, 16], F32)
        thr = p.sb("bthr", [128, 4], F32)
        cl = p.dma("sync", gall[:, :], g["gami"], inc="bl")
        p.wait("vector", "bl", cl)
        gv = gall[:, :].rearrange("p (r c) -> p r c", c=16).transpose([0, 2, 1])
        p.seq("vector", lambda e: e.tensor_reduce(gsum[:, :], gv, AX.X, ALU.add))
        cv = p.op("vector", lambda e: e.engine_nop(), inc="bv")
        p.wait("sync", "bv", cv)
        g["gbounce"] = p.nc.dram_tensor("gbounce", [1, 16], F32, kind="Internal").ap()
        p.dma("sync", g["gbounce"], gsum[0:1, :], inc="bl")
        cl = p.counts["bl"]
        p.wait("sync", "bl", cl)
        p.dma("sync", thr[:, :], bass.AP(g["gbounce"].tensor, 0, [[0, 128], [1, 4]]), inc="bl")
        cl = p.counts["bl"]
        p.wait("scalar", "bl", cl)
        for ci, nn in ((0, BH * DM), (1, BH * DM), (2, BH * DM)):
            p.seq("scalar", lambda e, ci=ci, nn=nn: e.activation(
                thr[:, ci:ci + 1], thr[:, ci:ci + 1], AF.Identity,
                scale=0.5 / nn, bias=cst[0:128, 2:3]))
        cs = p.op("scalar", lambda e: e.nop(), inc="bs")
        specs = [("bw1T_s", DMPC, BH, g["q1o"], 0),
                 ("bw2T_s", DMPC, BH, g["q2o"], 1),
                 ("bw3T_s", BHPC, DM, g["q3o"], 2)]
        first = True
        for nm, rows, cols, dstd, ci in specs:
            nt = cdiv(rows, 128)
            for it in range(nt):
                r = min(128, rows - 128 * it)
                if not first:
                    p.wait("sync", "be", p.counts["be"])
                first = False
                cl = p.dma("sync", wt[0:r, 0:cols], g[nm][128 * it:128 * it + r, :], inc="bl")
                p.wait("scalar", "bl", cl)
                p.wait("scalar", "bs", cs)
                p.seq("scalar", lambda e, r=r, cols=cols: e.activation(wa[0:r, 0:cols], wt[0:r, 0:cols], AF.Abs))
                c2 = p.op("scalar", lambda e, r=r, cols=cols: e.sign(sg[0:r, 0:cols], wt[0:r, 0:cols]), inc="bs")
                p.wait("vector", "bs", c2)
                p.seq("vector", lambda e, r=r, cols=cols, ci=ci: e.tensor_scalar(
                    wa[0:r, 0:cols], wa[0:r, 0:cols], thr[0:r, ci:ci + 1], None, ALU.is_gt))
                cv = p.op("vector", lambda e, r=r, cols=cols: e.tensor_mul(
                    qo[0:r, 0:cols], wa[0:r, 0:cols], sg[0:r, 0:cols]), inc="be")
                p.wait("sync", "be", cv)
                p.dma("sync", dstd[128 * it:128 * it + r, :], qo[0:r, 0:cols], inc="be")
        p.wait("sync", "be", p.counts["be"])
        p.wait("gpsimd", "be", p.counts["be"])
    p.barrier()


def emit_finale(p, g, cst, idf):
    for snm in ("fl", "fv", "ft", "fe"):
        p.sem(snm)
    with p.scope():
        xt = p.sb("fxt", [128, DM], F32)
        a1 = p.sb("fa1", [128, DM], F32)
        ot = p.sb("fot", [128, DM], F32)
        psf = [p.ps(f"fpsf{j}", [128, 128], F32) for j in range(2)]
        ev_f = {}
        for i in range(16):
            if i > 0:
                p.wait("sync", "fl", p.counts["fl"])
            p.dma("sync", xt[:, :], g["x1i"][128 * i:128 * (i + 1), :], inc="fl")
            cl = p.dma("sync", a1[:, :].rearrange("q (k c) -> q k c", c=128),
                       g["acc1"][:, 128 * i:128 * (i + 1)].rearrange("(k q) c -> q k c", q=128), inc="fl")
            p.wait("tensor", "fl", cl)
            for k in range(16):
                ps = psf[k % 2]
                if ev_f.get(k % 2) is not None:
                    p.wait("tensor", "fe", ev_f[k % 2])
                p.op("tensor", lambda e, k=k, ps=ps: e.transpose(
                    ps[:, :], a1[:, 128 * k:128 * (k + 1)], idf[:, :]), inc="ft")
                ct = p.counts["ft"]
                p.wait("vector", "ft", ct)
                ev_f[k % 2] = p.op("vector", lambda e, k=k, ps=ps: e.tensor_add(
                    ot[:, 128 * k:128 * (k + 1)], xt[:, 128 * k:128 * (k + 1)], ps[:, :]), inc="fe")
            ce = p.counts["fe"]
            p.wait("sync", "fe", ce)
            p.dma("sync", g["out"][128 * i:128 * (i + 1), :], ot[:, :], inc="fl")
        p.wait("sync", "fl", p.counts["fl"])
        p.wait("gpsimd", "fl", p.counts["fl"])


_DBG = {}


def kernel(**inputs):
    per_core = host_prep(inputs)
    asc = np.ascontiguousarray
    f32 = np.float32

    ncA, _ = build(dict(FULL))
    resA = run_bass_kernel_spmd(ncA, per_core, core_ids=list(range(NCORES)))
    A = resA.results
    _DBG["A"] = A

    xm_global = np.concatenate([A[c]["xmo"] for c in range(NCORES)], axis=1)
    gam_cat = np.concatenate([A[c]["gamo"].reshape(1, 16) for c in range(NCORES)], axis=1)

    ncB1, _ = build_B1()
    inB1 = []
    for c in range(NCORES):
        ch0 = c * CHPC
        d = {
            "xmch": asc(xm_global[ch0:ch0 + CHPC, :]),
            "convw_s": per_core[c]["convw_s"],
            "convb_s": per_core[c]["convb_s"],
            "xpwT_s": per_core[c]["xpwT_s"],
        }
        inB1.append(d)
    resB1 = run_bass_kernel_spmd(ncB1, inB1, core_ids=list(range(NCORES)))
    B1 = resB1.results
    _DBG["B1"] = B1
    xd_cat = np.concatenate([B1[c]["xdpo"] for c in range(NCORES)], axis=0)

    ncB2, _ = build_B2()
    inB2 = []
    for c in range(NCORES):
        d = {
            "xdai": xd_cat,
            "xci": B1[c]["xco"],
            "dtwT_s": per_core[c]["dtwT_s"],
            "dtb_s": per_core[c]["dtb_s"],
            "alog_s": per_core[c]["alog_s"],
            "dskip_s": per_core[c]["dskip_s"],
            "gami": gam_cat,
            "bw1T_s": per_core[c]["bw1T_s"],
            "bw2T_s": per_core[c]["bw2T_s"],
            "bw3T_s": per_core[c]["bw3T_s"],
            "consts": per_core[c]["consts"],
        }
        inB2.append(d)
    resB2 = run_bass_kernel_spmd(ncB2, inB2, core_ids=list(range(NCORES)))
    B2 = resB2.results
    _DBG["B2"] = B2
    y_global = np.concatenate([B2[c]["yo"] for c in range(NCORES)], axis=0)
    q1_full = np.concatenate([B2[c]["q1o"] for c in range(NCORES)], axis=0)
    q2_full = np.concatenate([B2[c]["q2o"] for c in range(NCORES)], axis=0)
    q3_full = np.concatenate([B2[c]["q3o"] for c in range(NCORES)], axis=0)

    ccfg = dict(FULL)
    if C_STAGE:
        ccfg["debug_out"] = C_STAGE
    ncC, _ = build_C(ccfg)
    inC = []
    for c in range(NCORES):
        t0 = c * TL
        d = {
            "ymy": asc(y_global[:, t0:t0 + TL]),
            "zsili": A[c]["zsilo"],
            "xn1i": A[c]["xn1o"],
            "x1i": A[c]["x1o"],
            "xn1bi": A[c]["xn1bo"],
            "q1i": q1_full, "q2i": q2_full, "q3i": q3_full,
            "outpT": per_core[c]["outpT"],
            "ffpT": per_core[c]["ffpT"],
            "ffoT": per_core[c]["ffoT"],
            "ln2w": per_core[c]["ln2w"],
            "ln2b": per_core[c]["ln2b"],
            "id_bf": per_core[c]["id_bf"],
            "id_f32": per_core[c]["id_f32"],
            "consts": per_core[c]["consts"],
        }
        inC.append(d)
    resC = run_bass_kernel_spmd(ncC, inC, core_ids=list(range(NCORES)))
    _DBG.update(C=resC.results, xm=xm_global, y=y_global,
                q1=q1_full, q2=q2_full, q3=q3_full, gam=gam_cat)
    if C_STAGE and C_STAGE != "blfull":
        return None
    outs = [resC.results[c]["out"] for c in range(NCORES)]

    y = np.empty((B * T, DM), np.float32)
    for c in range(NCORES):
        b = c // 4
        t0 = (c % 4) * TL
        y[b * T + t0:b * T + t0 + TL, :] = outs[c]
    return y.reshape(B, T, DM)
